# version 26
# baseline (speedup 1.0000x reference)
import contextlib
import numpy as np

import concourse.bass as bass
import concourse.mybir as mybir
from concourse.bass_utils import run_bass_kernel_spmd

F32 = mybir.dt.float32
BF16 = mybir.dt.bfloat16
AF = mybir.ActivationFunctionType
ALU = mybir.AluOpType

D = 1024
HID = 4096
B = 4
SEQ = 4096
CH = 1024
CHH = CH + 3
NCG = D // 128
NH = HID // 128
N_CORES = 8
EPS_RMS = 1e-6
EPS_LN = 1e-5
GC = 0.7978845608028654
GA = 0.044715

G_HALO = [(0, 3), (3, 515), (515, 1027)]
G_MAIN = [(0, 512), (512, 1024)]


class Op:
    __slots__ = ("engine", "emit", "reads", "writes", "kind", "needs_inc",
                 "inc_val", "waits", "dma_sem", "dma_val", "dma_stream", "seq")

    def __init__(self, engine, emit, reads, writes, kind):
        self.engine = engine
        self.emit = emit
        self.reads = reads
        self.writes = writes
        self.kind = kind
        self.needs_inc = False
        self.inc_val = None
        self.waits = []
        self.dma_sem = None
        self.dma_val = None


class Plan:
    def __init__(self):
        self.ops = {e: [] for e in ("pe", "v", "s", "p", "q")}
        self.last_writer = {}
        self.readers = {}
        self.since_drain = {"v": set(), "s": set()}
        self.eng_sems = {}
        self.dma_sems = {}
        self.dma_counts = {}

    def _dep(self, op, dep):
        if dep is op:
            return
        if dep.kind == "coll":
            op.waits.append(("sem", dep.dma_sem, dep.dma_val))
            return
        if dep.kind == "dma":
            op.waits.append(("sem", self.dma_sems[dep.dma_stream],
                             self.dma_counts[dep.dma_stream]))
            return
        if dep.engine == op.engine:
            if op.engine == "pe":
                return
            if op.kind in ("dma", "coll") or op.engine == "p":
                op.waits.append(dep)
                return
            if id(dep) in self.since_drain.get(op.engine, set()):
                d = Op(op.engine, None, (), (), "drain")
                self.ops[op.engine].append(d)
                self.since_drain[op.engine] = set()
            return
        op.waits.append(dep)

    def add(self, engine, emit, reads=(), writes=(), kind="compute",
            dma_sem=None):
        op = Op(engine, emit, tuple(reads), tuple(writes), kind)
        for k in op.reads:
            lw = self.last_writer.get(k)
            if lw is not None:
                self._dep(op, lw)
        for k in op.writes:
            lw = self.last_writer.get(k)
            if lw is not None:
                self._dep(op, lw)
            for r in self.readers.get(k, ()):
                self._dep(op, r)
        if kind == "dma":
            op.dma_stream = dma_sem
            self.dma_counts[dma_sem] = self.dma_counts.get(dma_sem, 0) + 16
            op.dma_sem = self.dma_sems[dma_sem]
            op.dma_val = self.dma_counts[dma_sem]
        op.seq = len(self.ops[engine])
        self.ops[engine].append(op)
        for k in op.writes:
            self.last_writer[k] = op
            self.readers[k] = []
        if engine in ("v", "s") and kind == "compute":
            self.since_drain[engine].add(id(op))
        for k in op.reads:
            self.readers.setdefault(k, []).append(op)
        return op

    def number(self):
        for e in self.ops:
            for op in self.ops[e]:
                best = {}
                rest = []
                for dep in op.waits:
                    if isinstance(dep, tuple):
                        rest.append(dep)
                        continue
                    b = best.get(dep.engine)
                    if b is None or dep.seq > b.seq:
                        best[dep.engine] = dep
                for dep in best.values():
                    dep.needs_inc = True
                op.waits = rest + list(best.values())
        for e in self.ops:
            cnt = 0
            for op in self.ops[e]:
                if op.needs_inc and op.kind == "compute":
                    cnt += 1
                    op.inc_val = cnt

    def emit_engine(self, ename, eng):
        hwm = {}
        for op in self.ops[ename]:
            if op.kind == "drain":
                eng.drain()
                continue
            for dep in op.waits:
                if isinstance(dep, tuple):
                    _, sem, val = dep
                else:
                    sem, val = self.eng_sems[dep.engine], dep.inc_val
                    assert val is not None
                if hwm.get(id(sem), -1) < val:
                    eng.wait_ge(sem, val)
                    hwm[id(sem)] = val
            inst = op.emit(eng)
            if op.kind == "dma":
                inst.then_inc(op.dma_sem, 16)
            elif op.kind == "coll":
                inst.then_inc(op.dma_sem, op.dma_val)
            elif op.needs_inc:
                inst.then_inc(self.eng_sems[op.engine], 1)


class Region:
    PAGE = 512

    def __init__(self, tensor, name, n_elems):
        self.t = tensor
        self.name = name
        self.n = n_elems

    def v(self, c0, c1):
        return self.t[:, c0:c1]

    def k(self, c0, c1):
        return tuple(f"{self.name}:{i}"
                     for i in range(c0 // self.PAGE, (c1 + self.PAGE - 1) // self.PAGE))

    def kall(self):
        return self.k(0, self.n)


def build_nc():
    nc = bass.Bass("TRN2", target_bir_lowering=False)

    def din(name, shape, dt=F32):
        return nc.dram_tensor(name, shape, dt, kind="ExternalInput").ap()

    xs = [din("x0", [D, CHH]), din("x1", [D, CHH])]
    wdr = {n: din(n, [NCG * 128, D]) for n in ("wx", "wy", "wa", "wi", "wo")}
    wdr.update({n: din(n, [NH * 128, D]) for n in ("wg", "wu", "w1")})
    wdr.update({n: din(n, [NCG * 128, HID]) for n in ("wd", "w2")})
    norm1w = din("norm1w", [128, NCG])
    norm2w = din("norm2w", [128, NCG])
    convb = din("convb", [128, NCG])
    convw = din("convw", [128, 4 * NCG])
    lam = din("lam", [128, NCG])
    b1v = din("b1v", [128, NH])
    b2v = din("b2v", [128, NCG])
    mask_e = din("mask_e", [128, 128])
    mask_o = din("mask_o", [128, 128])
    slf = din("slf", [128, 2])
    ident = din("ident", [128, 128])

    ys = [nc.dram_tensor(f"y{i}", [D, CH], F32, kind="ExternalOutput").ap()
          for i in range(2)]
    ag_in = [nc.dram_tensor(f"ag{i}_in", [16, 128], F32) for i in range(2)]
    ag_out = [nc.dram_tensor(f"ag{i}_out", [128, 128], F32) for i in range(2)]

    with contextlib.ExitStack() as ctx:
        E = ctx.enter_context

        rg1_t = E(nc.sbuf_tensor("rg1", [128, NCG * CHH], F32))
        rg2_t = E(nc.sbuf_tensor("rg2", [128, NCG * CH], F32))
        rg3_t = E(nc.sbuf_tensor("rg3", [128, NCG * CH], F32))
        rg5_t = E(nc.sbuf_tensor("rg5", [128, NCG * CHH + 3 * NCG * CH], BF16))
        rg6_t = E(nc.sbuf_tensor("rg6", [128, NCG * CHH], BF16))
        R1 = Region(rg1_t, "rg1", NCG * CHH)
        R2 = Region(rg2_t, "rg2", NCG * CH)
        R3 = Region(rg3_t, "rg3", NCG * CH)
        R5 = Region(rg5_t, "rg5", NCG * CHH + 3 * NCG * CH)
        R5f = Region(rg5_t.ap().bitcast(F32), "rg5", (NCG * CHH + 3 * NCG * CH) // 2)
        R5f.PAGE = 256
        R3b = Region(rg3_t.ap().bitcast(BF16), "rg3", 2 * NCG * CH)
        R3b.PAGE = 1024
        R6 = Region(rg6_t, "rg6", NCG * CHH)

        scr_t = [E(nc.sbuf_tensor(f"scr{i}", [128, CH], F32)) for i in range(3)]
        wsm = [E(nc.sbuf_tensor(f"wsm{i}", [128, D], BF16)) for i in range(3)]
        wbg = [E(nc.sbuf_tensor(f"wbg{i}", [128, 2048], BF16)) for i in range(2)]

        sv = {}
        for name, dram, wdt in [
            ("norm1w", norm1w, NCG), ("norm2w", norm2w, NCG),
            ("convb", convb, NCG), ("convw", convw, 4 * NCG),
            ("lam", lam, NCG), ("b1v", b1v, NH), ("b2v", b2v, NCG),
            ("mask_e", mask_e, 128), ("mask_o", mask_o, 128),
            ("slf", slf, 2), ("ident", ident, 128),
        ]:
            sv[name] = (E(nc.sbuf_tensor("sv_" + name, [128, wdt], F32)), dram)
        sv_gneg = E(nc.sbuf_tensor("sv_gneg", [128, NCG], F32))
        CONST_VALS = [1.0, -1.0, 0.5, -0.5, 2.0, EPS_RMS, EPS_LN,
                      2.0 * GC, 2.0 * GC * GA]
        constv = E(nc.sbuf_tensor("constv", [128, len(CONST_VALS)], F32))
        for _i, _val in enumerate(CONST_VALS):
            nc.const_aps.register(F32, _val, constv[:, _i:_i + 1])
        inv1024 = E(nc.sbuf_tensor("inv1024", [128, 1], F32))
        inv1024b = E(nc.sbuf_tensor("inv1024b", [128, 1], BF16))
        ones_row = E(nc.sbuf_tensor("ones_row", [1, 128], F32))
        ends = E(nc.sbuf_tensor("ends", [128, 16], F32))
        foldE = E(nc.sbuf_tensor("foldE", [128, 16], F32))
        foldO = E(nc.sbuf_tensor("foldO", [128, 16], F32))
        H2 = E(nc.sbuf_tensor("H2", [128, NCG], F32))
        hin = E(nc.sbuf_tensor("hin", [128, NCG], F32))
        tiny = [E(nc.sbuf_tensor(f"tiny{i}", [128, NCG], F32)) for i in range(2)]

        ps = [E(nc.psum_tensor(f"ps{i}", [128, 512], F32)) for i in range(8)]

        sem_pe = E(nc.semaphore("sem_pe"))
        sem_v = E(nc.semaphore("sem_v"))
        sem_s = E(nc.semaphore("sem_s"))
        sem_p = E(nc.semaphore("sem_p"))
        DSTREAMS = (["init", "x", "ag0", "ag1", "y0", "y1"]
                    + [f"wsm{i}" for i in range(3)]
                    + [f"wbg{i}" for i in range(2)])
        dsem = {n: E(nc.semaphore("d_" + n)) for n in DSTREAMS}
        cc = [E(nc.semaphore(f"cc{i}")) for i in range(2)]

        P = Plan()
        P.eng_sems = {"pe": sem_pe, "v": sem_v, "s": sem_s, "p": sem_p}
        P.dma_sems = dsem
        add = P.add

        def x_v(cg, a, b):
            return R1.v(cg * CHH + a, cg * CHH + b)

        def x_k(cg, a, b):
            return R1.k(cg * CHH + a, cg * CHH + b)

        def h_v(cg, a=0, b=CH):
            return R2.v(cg * CH + a, cg * CH + b)

        def h_k(cg, a=0, b=CH):
            return R2.k(cg * CH + a, cg * CH + b)

        def ln1_v(cg, a=0, b=CH):
            return R3.v(cg * CH + a, cg * CH + b)

        def ln1_k(cg, a=0, b=CH):
            return R3.k(cg * CH + a, cg * CH + b)

        def A_v(cg):
            return R3b.v(cg * CH, (cg + 1) * CH)

        def A_k(cg):
            return R3b.k(cg * CH, (cg + 1) * CH)

        OXB = 0
        OXBC = NCG * CHH
        OLA = OXBC + NCG * CH
        OIXB = OLA + NCG * CH
        ROWB = OLA // 2
        ROWB2 = ROWB + 1028
        ROWT = ROWB2 + 1024

        def rowb_v(a, b):
            return R5f.v(ROWB + a, ROWB + b)[0:1, :]

        def rowb_k(a, b):
            return R5f.k(ROWB + a, ROWB + b)

        def rowb2_v(a, b):
            return R5f.v(ROWB2 + a, ROWB2 + b)[0:1, :]

        def rowb2_k(a, b):
            return R5f.k(ROWB2 + a, ROWB2 + b)

        def rowt_v(a, b):
            return R5f.v(ROWT + a, ROWT + b)[0:1, :]

        def rowt_k(a, b):
            return R5f.k(ROWT + a, ROWT + b)

        def xb_v(cg, a, b):
            return R5.v(OXB + cg * CHH + a, OXB + cg * CHH + b)

        def xb_k(cg, a, b):
            return R5.k(OXB + cg * CHH + a, OXB + cg * CHH + b)

        def hyb_v(cg, a=0, b=CH):
            return R5.v(OXBC + cg * CH + a, OXBC + cg * CH + b)

        def hyb_k(cg, a=0, b=CH):
            return R5.k(OXBC + cg * CH + a, OXBC + cg * CH + b)

        def xbc_v(cg, a=0, b=CH):
            return R5.v(OXBC + cg * CH + a, OXBC + cg * CH + b)

        def xbc_k(cg, a=0, b=CH):
            return R5.k(OXBC + cg * CH + a, OXBC + cg * CH + b)

        def la_v(cg):
            return R5.v(OLA + cg * CH, OLA + (cg + 1) * CH)

        def la_k(cg):
            return R5.k(OLA + cg * CH, OLA + (cg + 1) * CH)

        def ixb_v(cg):
            return R5.v(OIXB + cg * CH, OIXB + (cg + 1) * CH)

        def ixb_k(cg):
            return R5.k(OIXB + cg * CH, OIXB + (cg + 1) * CH)

        def yb2_v(cg):
            return R5.v(cg * CH, (cg + 1) * CH)

        def yb2_k(cg):
            return R5.k(cg * CH, (cg + 1) * CH)

        def _pj(j):
            return j * CH if j < 8 else OXBC + (j - 8) * CH

        def prod_v(j, a=0, b=CH):
            return R5.v(_pj(j) + a, _pj(j) + b)

        def prod_k(j, a=0, b=CH):
            return R5.k(_pj(j) + a, _pj(j) + b)

        def t_v(cg, a, b):
            return R6.v(cg * CHH + a, cg * CHH + b)

        def t_k(cg, a, b):
            return R6.k(cg * CHH + a, cg * CHH + b)

        def u_v(cg, a=0, b=CH):
            return R6.v(cg * CH + a, cg * CH + b)

        def u_k(cg, a=0, b=CH):
            return R6.k(cg * CH + a, cg * CH + b)

        def K(name, i=0):
            return (f"{name}:{i}",)

        def KP(i):
            return (f"psum:{i}",)

        for name, (sb, dram) in sv.items():
            add("q", (lambda e, d=dram, s=sb: e.dma_start(out=s[:, :], in_=d)),
                writes=K(name), kind="dma", dma_sem="init")
        for _i, _val in enumerate(CONST_VALS):
            add("v", (lambda e, i=_i, val=_val:
                      e.memset(constv[:, i:i + 1], val)), writes=K("consts"))
        add("v", lambda e: e.memset(inv1024[:, :], 1.0 / 1024.0),
            writes=K("inv1024"))
        add("v", lambda e: e.memset(inv1024b[:, :], 1.0 / 1024.0),
            writes=K("inv1024b"))
        add("v", lambda e: e.memset(ones_row[:, :], 1.0), writes=K("ones_row"))

        add("s", lambda e: e.activation(tiny[0][:, :], sv["lam"][0][:, :], AF.Abs),
            reads=K("lam"), writes=K("tiny", 0))
        add("s", lambda e: e.activation(tiny[0][:, :], tiny[0][:, :], AF.Exp, scale=-1.0),
            reads=K("tiny", 0) + K("consts"), writes=K("tiny", 0))
        add("s", lambda e: e.activation(tiny[0][:, :], tiny[0][:, :], AF.Ln, bias=1.0),
            reads=K("tiny", 0) + K("consts"), writes=K("tiny", 0))
        add("s", lambda e: e.activation(tiny[1][:, :], sv["lam"][0][:, :], AF.Relu),
            reads=K("lam"), writes=K("tiny", 1))
        add("v", lambda e: e.tensor_tensor(tiny[0][:, :], tiny[0][:, :], tiny[1][:, :], ALU.add),
            reads=K("tiny", 0) + K("tiny", 1), writes=K("tiny", 0))
        add("v", lambda e: e.tensor_scalar(sv_gneg[:, :], tiny[0][:, :], -8.0, None, ALU.mult),
            reads=K("tiny", 0), writes=K("gneg"))

        st = {"sm": 0, "bg": 0, "ps": 0, "sc": 0}

        def slab_small(wname, mc):
            slot = st["sm"] % len(wsm)
            st["sm"] += 1
            d = wdr[wname]
            add("p", (lambda e, d=d, b=wsm[slot], m=mc:
                      e.dma_start(out=b[:, :], in_=d[m * 128:(m + 1) * 128, :])),
                writes=K("wsm", slot), kind="dma", dma_sem=f"wsm{slot}")
            return slot

        def slab_big(wname, mc, half):
            slot = st["bg"] % len(wbg)
            st["bg"] += 1
            d = wdr[wname]
            add("p", (lambda e, d=d, b=wbg[slot], m=mc, h=half:
                      e.dma_start(out=b[:, :],
                                  in_=d[m * 128:(m + 1) * 128, h * 2048:(h + 1) * 2048])),
                writes=K("wbg", slot), kind="dma", dma_sem=f"wbg{slot}")
            return slot

        st["resv"] = set()

        def ps_next():
            while True:
                i = st["ps"] % 8
                st["ps"] += 1
                if i not in st["resv"]:
                    return i

        def ps_reserve():
            i = ps_next()
            st["resv"].add(i)
            return i

        def ps_release(i):
            st["resv"].discard(i)

        def _mk_scr(i):
            return (lambda a=0, b=CH: scr_t[i][:, a:b],
                    lambda a=0, b=CH: (f"scr{i}:{a // 512}",)
                    if b - a <= 512 else (f"scr{i}:0", f"scr{i}:1"))

        XCF = OXBC // 2
        LAF = OLA // 2
        IXF = OIXB // 2

        def _mk_slot(base, i):
            return (lambda a=0, b=CH: R5f.v(base + i * CH + a, base + i * CH + b),
                    lambda a=0, b=CH: R5f.k(base + i * CH + a, base + i * CH + b))

        SCR_MAIN = [_mk_scr(i) for i in range(3)]

        rg6f = rg6_t.ap().bitcast(F32)

        def _mk_r6(i):
            base = 512 + i * CH
            return (lambda a=0, b=CH: rg6f[:, base + a:base + b],
                    lambda a=0, b=CH: R6.k(2 * (base + a), 2 * (base + b)))
        SCR_R6 = [_mk_r6(i) for i in range(3)]
        SCR_XC = [_mk_slot(XCF, i) for i in range(4)]
        SCR_LA = [_mk_slot(LAF, i) for i in range(4)]
        SCR_IX = [_mk_slot(IXF, i) for i in range(4)]

        endsT = scr_t[0][0:16, 0:128]
        K_endsT = ("scr0:0",)
        gT = scr_t[2][:, 0:128]
        gAll = scr_t[2][:, 0:128]
        K_gTall = ("scr2:0",)
        fsc = [scr_t[0][:, 0:128], scr_t[1][:, 0:128]]
        K_fsc = [("scr0:0",), ("scr1:0",)]

        pool_state = {"pool": SCR_MAIN, "i": 0}

        def set_pool(p):
            pool_state["pool"] = p
            pool_state["i"] = 0

        def scr_next():
            p = pool_state["pool"]
            i = pool_state["i"] % len(p)
            pool_state["i"] += 1
            return p[i]

        def mm_group(wkey, wbuf, nk, rhs_v, rhs_k, groups, pss):
            for ki in range(nk):
                lhs = wbuf[:, ki * 128:(ki + 1) * 128]
                for gi, (c0, c1) in enumerate(groups):
                    pi = pss[gi]
                    add("pe",
                        (lambda e, p=ps[pi], l=lhs, r=rhs_v(ki, c0, c1),
                         w=c1 - c0, stt=(ki == 0), sp=(ki == nk - 1):
                         e.matmul(p[:, 0:w], l, r, start=stt, stop=sp)),
                        reads=wkey + rhs_k(ki, c0, c1),
                        writes=KP(pi))

        def norm_stats_begin(want_mean):
            stn = {"sq": [ps_reserve() for _ in G_MAIN]}
            if want_mean:
                stn["mn"] = [ps_reserve() for _ in G_MAIN]
            return stn

        def norm_stats_cg(stn, cg, src_v, src_k, first, last):
            for gi, (c0, c1) in enumerate(G_MAIN):
                w = c1 - c0
                sv_, sk_ = scr_next()
                sqb = sv_(0, w).bitcast(BF16)[:, 0:w]
                add("s", (lambda e, sc=sqb, sr=src_v(cg, c0, c1):
                          e.activation(sc, sr, AF.Square)),
                    reads=src_k(cg, c0, c1), writes=sk_(0, w))
                add("pe", (lambda e, p=ps[stn["sq"][gi]], sc=sqb,
                           stt=first, sp=last, w=w:
                           e.matmul(p[0:1, 0:w], inv1024b[:, :], sc,
                                    start=stt, stop=sp)),
                    reads=sk_(0, w) + K("inv1024b"), writes=KP(stn["sq"][gi]))
                if "mn" in stn:
                    add("pe", (lambda e, p=ps[stn["mn"][gi]],
                               sr=src_v(cg, c0, c1), stt=first, sp=last, w=w:
                               e.matmul(p[0:1, 0:w], inv1024[:, :], sr,
                                        start=stt, stop=sp)),
                        reads=src_k(cg, c0, c1) + K("inv1024"),
                        writes=KP(stn["mn"][gi]))

        def rms_finish(stn, src_v, src_k, wvec, wkey, dst_v, dst_k, eps):
            for gi, (c0, c1) in enumerate(G_MAIN):
                w = c1 - c0
                pstat = stn["sq"][gi]
                add("s", (lambda e, p=ps[pstat], r=rowb_v(c0, c1), w=w:
                          e.activation(r, p[0:1, 0:w], AF.Ln, bias=eps)),
                    reads=KP(pstat) + K("consts"), writes=rowb_k(c0, c1))
                add("s", (lambda e, r=rowb_v(c0, c1):
                          e.activation(r, r, AF.Exp, scale=-0.5)),
                    reads=rowb_k(c0, c1) + K("consts"), writes=rowb_k(c0, c1))
                pbc = ps_next()
                add("pe", (lambda e, p=ps[pbc], r=rowb_v(c0, c1), w=w:
                           e.matmul(p[:, 0:w], ones_row[:, :],
                                    r, start=True, stop=True)),
                    reads=rowb_k(c0, c1) + K("ones_row"), writes=KP(pbc))
                for cg in range(NCG):
                    add("v", (lambda e, p=ps[pbc], sr=src_v(cg, c0, c1),
                              d=dst_v(cg, c0, c1), wv=wvec[:, cg:cg + 1], w=w:
                              e.scalar_tensor_tensor(d, sr, wv, p[:, 0:w],
                                                     ALU.mult, ALU.mult)),
                        reads=src_k(cg, c0, c1) + KP(pbc) + wkey,
                        writes=dst_k(cg, c0, c1))
                ps_release(pstat)

        def ln_finish(stn, src_v, src_k, dst_v, dst_k, eps, extra=None,
                      stream_out=None):
            bcs = []
            for gi, (c0, c1) in enumerate(G_MAIN):
                w = c1 - c0
                psq = stn["sq"][gi]
                psm = stn["mn"][gi]
                add("s", (lambda e, p=ps[psm], r=rowt_v(c0, c0 + w), w=w:
                          e.activation(r, p[0:1, 0:w], AF.Square)),
                    reads=KP(psm), writes=rowt_k(c0, c1))
                add("s", (lambda e, p=ps[psm], r=rowb2_v(c0, c1), w=w:
                          e.copy(r, p[0:1, 0:w])),
                    reads=KP(psm), writes=rowb2_k(c0, c1))
                add("v", (lambda e, p=ps[psq], r=rowb_v(c0, c1),
                          rt=rowt_v(c0, c1), w=w:
                          e.tensor_tensor(r, p[0:1, 0:w], rt, ALU.subtract)),
                    reads=KP(psq) + rowt_k(c0, c1), writes=rowb_k(c0, c1))
                add("s", (lambda e, r=rowb_v(c0, c1):
                          e.activation(r, r, AF.Ln, bias=eps)),
                    reads=rowb_k(c0, c1) + K("consts"), writes=rowb_k(c0, c1))
                add("s", (lambda e, r=rowb_v(c0, c1):
                          e.activation(r, r, AF.Exp, scale=-0.5)),
                    reads=rowb_k(c0, c1) + K("consts"), writes=rowb_k(c0, c1))
                add("v", (lambda e, r=rowb_v(c0, c1), r2=rowb2_v(c0, c1):
                          e.tensor_tensor(r2, r2, r, ALU.mult)),
                    reads=rowb_k(c0, c1) + rowb2_k(c0, c1), writes=rowb2_k(c0, c1))
                pb1 = ps_next()
                pb2 = ps_next()
                add("pe", (lambda e, p=ps[pb1], r=rowb_v(c0, c1), w=w:
                           e.matmul(p[:, 0:w], ones_row[:, :], r,
                                    start=True, stop=True)),
                    reads=rowb_k(c0, c1) + K("ones_row"), writes=KP(pb1))
                add("pe", (lambda e, p=ps[pb2], r=rowb2_v(c0, c1), w=w:
                           e.matmul(p[:, 0:w], ones_row[:, :], r,
                                    start=True, stop=True)),
                    reads=rowb2_k(c0, c1) + K("ones_row"), writes=KP(pb2))
                bcs.append((pb1, pb2))
                ps_release(psq)
                ps_release(psm)
            sweep = NCG if stream_out is None else 4
            for base in range(0, NCG, sweep):
                cgs = list(range(base, base + sweep))
                for gi, (c0, c1) in enumerate(G_MAIN):
                    pb1, _ = bcs[gi]
                    w = c1 - c0
                    for cg in cgs:
                        add("v", (lambda e, p=ps[pb1], sr=src_v(cg, c0, c1),
                                  d=dst_v(cg, c0, c1), w=w:
                                  e.tensor_tensor(d, sr, p[:, 0:w], ALU.mult)),
                            reads=src_k(cg, c0, c1) + KP(pb1),
                            writes=dst_k(cg, c0, c1))
                for gi, (c0, c1) in enumerate(G_MAIN):
                    _, pb2 = bcs[gi]
                    w = c1 - c0
                    for cg in cgs:
                        add("v", (lambda e, p=ps[pb2], d=dst_v(cg, c0, c1), w=w:
                                  e.tensor_tensor(d, d, p[:, 0:w], ALU.subtract)),
                            reads=dst_k(cg, c0, c1) + KP(pb2),
                            writes=dst_k(cg, c0, c1))
                for cg in cgs:
                    if extra is not None:
                        ev, ek = extra
                        for (c0, c1) in G_MAIN:
                            add("s", (lambda e, d=dst_v(cg, c0, c1),
                                      g=ev(cg, c0, c1):
                                      e.copy(g, d)),
                                reads=dst_k(cg, c0, c1), writes=ek(cg, c0, c1))
                    if stream_out is not None:
                        add("q", (lambda e, cg=cg, d=stream_out:
                                  e.dma_start(out=d[cg * 128:(cg + 1) * 128, :],
                                              in_=dst_v(cg, 0, CH))),
                            reads=dst_k(cg, 0, CH), kind="dma",
                            dma_sem=f"y{cg % 2}")
        def rms_norm(src_v, src_k, wvec, wkey, dst_v, dst_k, groups, eps):
            for (c0, c1) in groups:
                w = c1 - c0
                pstat = ps_next()
                for cg in range(NCG):
                    sv_, sk_ = scr_next()
                    sqb = sv_(0, w).bitcast(BF16)[:, 0:w]
                    add("s", (lambda e, sc=sqb, s=src_v(cg, c0, c1):
                              e.activation(sc, s, AF.Square)),
                        reads=src_k(cg, c0, c1), writes=sk_(0, w))
                    add("pe", (lambda e, p=ps[pstat], sc=sqb,
                               stt=(cg == 0), sp=(cg == NCG - 1), w=w:
                               e.matmul(p[0:1, 0:w], inv1024b[:, :], sc,
                                        start=stt, stop=sp)),
                        reads=sk_(0, w) + K("inv1024b"), writes=KP(pstat))
                add("s", (lambda e, p=ps[pstat], r=rowb_v(c0, c1), w=w:
                          e.activation(r, p[0:1, 0:w], AF.Ln, bias=eps)),
                    reads=KP(pstat) + K("consts"), writes=rowb_k(c0, c1))
                add("s", (lambda e, r=rowb_v(c0, c1):
                          e.activation(r, r, AF.Exp, scale=-0.5)),
                    reads=rowb_k(c0, c1) + K("consts"), writes=rowb_k(c0, c1))
                pbc = ps_next()
                add("pe", (lambda e, p=ps[pbc], r=rowb_v(c0, c1), w=w:
                           e.matmul(p[:, 0:w], ones_row[:, :],
                                    r, start=True, stop=True)),
                    reads=rowb_k(c0, c1) + K("ones_row"), writes=KP(pbc))
                for cg in range(NCG):
                    add("v", (lambda e, p=ps[pbc], s=src_v(cg, c0, c1),
                              d=dst_v(cg, c0, c1), wv=wvec[:, cg:cg + 1], w=w:
                              e.scalar_tensor_tensor(d, s, wv, p[:, 0:w],
                                                     ALU.mult, ALU.mult)),
                        reads=src_k(cg, c0, c1) + KP(pbc) + wkey,
                        writes=dst_k(cg, c0, c1))

        def layer_norm(src_v, src_k, dst_v, dst_k, eps, extra=None,
                       stream_out=None, src2_v=None, src2_k=None):
            if src2_v is not None:
                src_v, src_k = src2_v, src2_k
            bcs = []
            for (c0, c1) in G_MAIN:
                w = c1 - c0
                psq = ps_next()
                psm = ps_next()
                for cg in range(NCG):
                    sv_, sk_ = scr_next()
                    sqb = sv_(0, w).bitcast(BF16)[:, 0:w]
                    add("s", (lambda e, sc=sqb, s=src_v(cg, c0, c1):
                              e.activation(sc, s, AF.Square)),
                        reads=src_k(cg, c0, c1), writes=sk_(0, w))
                    add("pe", (lambda e, p=ps[psq], sc=sqb,
                               stt=(cg == 0), sp=(cg == NCG - 1), w=w:
                               e.matmul(p[0:1, 0:w], inv1024b[:, :], sc,
                                        start=stt, stop=sp)),
                        reads=sk_(0, w) + K("inv1024b"), writes=KP(psq))
                    add("pe", (lambda e, p=ps[psm], s=src_v(cg, c0, c1),
                               stt=(cg == 0), sp=(cg == NCG - 1), w=w:
                               e.matmul(p[0:1, 0:w], inv1024[:, :], s,
                                        start=stt, stop=sp)),
                        reads=src_k(cg, c0, c1) + K("inv1024"), writes=KP(psm))
                add("s", (lambda e, p=ps[psm], r=rowt_v(c0, c0 + w), w=w:
                          e.activation(r, p[0:1, 0:w], AF.Square)),
                    reads=KP(psm), writes=rowt_k(c0, c1))
                add("s", (lambda e, p=ps[psm], r=rowb2_v(c0, c1), w=w:
                          e.copy(r, p[0:1, 0:w])),
                    reads=KP(psm), writes=rowb2_k(c0, c1))
                add("v", (lambda e, p=ps[psq], r=rowb_v(c0, c1),
                          rt=rowt_v(c0, c1), w=w:
                          e.tensor_tensor(r, p[0:1, 0:w], rt, ALU.subtract)),
                    reads=KP(psq) + rowt_k(c0, c1), writes=rowb_k(c0, c1))
                add("s", (lambda e, r=rowb_v(c0, c1):
                          e.activation(r, r, AF.Ln, bias=eps)),
                    reads=rowb_k(c0, c1) + K("consts"), writes=rowb_k(c0, c1))
                add("s", (lambda e, r=rowb_v(c0, c1):
                          e.activation(r, r, AF.Exp, scale=-0.5)),
                    reads=rowb_k(c0, c1) + K("consts"), writes=rowb_k(c0, c1))
                add("v", (lambda e, r=rowb_v(c0, c1), r2=rowb2_v(c0, c1):
                          e.tensor_tensor(r2, r2, r, ALU.mult)),
                    reads=rowb_k(c0, c1) + rowb2_k(c0, c1), writes=rowb2_k(c0, c1))
                pb1 = ps_next()
                pb2 = ps_next()
                add("pe", (lambda e, p=ps[pb1], r=rowb_v(c0, c1), w=w:
                           e.matmul(p[:, 0:w], ones_row[:, :], r,
                                    start=True, stop=True)),
                    reads=rowb_k(c0, c1) + K("ones_row"), writes=KP(pb1))
                add("pe", (lambda e, p=ps[pb2], r=rowb2_v(c0, c1), w=w:
                           e.matmul(p[:, 0:w], ones_row[:, :], r,
                                    start=True, stop=True)),
                    reads=rowb2_k(c0, c1) + K("ones_row"), writes=KP(pb2))
                bcs.append((pb1, pb2))
            sweep = NCG if stream_out is None else 4
            for base in range(0, NCG, sweep):
                cgs = list(range(base, base + sweep))
                for gi, (c0, c1) in enumerate(G_MAIN):
                    pb1, _ = bcs[gi]
                    w = c1 - c0
                    for cg in cgs:
                        add("v", (lambda e, p=ps[pb1], s=src_v(cg, c0, c1),
                                  d=dst_v(cg, c0, c1), w=w:
                                  e.tensor_tensor(d, s, p[:, 0:w], ALU.mult)),
                            reads=src_k(cg, c0, c1) + KP(pb1),
                            writes=dst_k(cg, c0, c1))
                for gi, (c0, c1) in enumerate(G_MAIN):
                    _, pb2 = bcs[gi]
                    w = c1 - c0
                    for cg in cgs:
                        add("v", (lambda e, p=ps[pb2], d=dst_v(cg, c0, c1), w=w:
                                  e.tensor_tensor(d, d, p[:, 0:w], ALU.subtract)),
                            reads=dst_k(cg, c0, c1) + KP(pb2),
                            writes=dst_k(cg, c0, c1))
                for cg in cgs:
                    if extra is not None:
                        ev, ek = extra
                        for (c0, c1) in G_MAIN:
                            add("s", (lambda e, d=dst_v(cg, c0, c1),
                                      g=ev(cg, c0, c1):
                                      e.copy(g, d)),
                                reads=dst_k(cg, c0, c1), writes=ek(cg, c0, c1))
                    if stream_out is not None:
                        add("q", (lambda e, cg=cg, d=stream_out:
                                  e.dma_start(out=d[cg * 128:(cg + 1) * 128, :],
                                              in_=dst_v(cg, 0, CH))),
                            reads=dst_k(cg, 0, CH), kind="dma",
                            dma_sem=f"y{cg % 2}")

        def pass_head(pi):
            x_dram = xs[pi]
            add("q", (lambda e, d=x_dram:
                      e.dma_start(
                          out=rg1_t.ap().rearrange("p (g n) -> p g n", g=NCG, n=CHH),
                          in_=d.rearrange("(g p) n -> p g n", p=128, g=NCG))),
                writes=R1.kall(), kind="dma", dma_sem="x")
            set_pool(SCR_MAIN)
            rms_norm(x_v, x_k, sv["norm1w"][0], K("norm1w"),
                     t_v, t_k, G_HALO, EPS_RMS)

        def do_pass(pi, head_next=None):
            x_dram = xs[pi]

            for mc in range(NCG):
                slot = slab_small("wx", mc)
                p3 = [ps_next(), ps_next(), ps_next()]
                mm_group(K("wsm", slot), wsm[slot], NCG, t_v, t_k, G_HALO, p3)
                for gi, (c0, c1) in enumerate(G_HALO):
                    add("s", (lambda e, p=ps[p3[gi]], d=xb_v(mc, c0, c1), w=c1 - c0:
                              e.copy(d, p[:, 0:w])),
                        reads=KP(p3[gi]),
                        writes=tuple(k + f":g{gi}" for k in xb_k(mc, c0, c1)))

            accs = SCR_LA + SCR_IX
            wy_state = {"m": 0}

            def wy_group():
                mc = wy_state["m"]
                wy_state["m"] += 1
                slot = slab_small("wy", mc)
                p2 = [ps_next(), ps_next()]
                mm_group(K("wsm", slot), wsm[slot], NCG, t_v, t_k,
                         [(3, 515), (515, 1027)], p2)
                for gi, (c0, c1) in enumerate(G_MAIN):
                    add("s", (lambda e, p=ps[p2[gi]], d=yb2_v(mc)[:, c0:c1],
                              w=c1 - c0:
                              e.copy(d, p[:, 0:w])),
                        reads=KP(p2[gi]), writes=yb2_k(mc))

            def conv_sweep(k):
                for cg in range(NCG):
                    xbread = xb_k(cg, 0, CHH) + tuple(
                        k + f":g{gi}" for gi in range(3)
                        for k in xb_k(cg, 0, CHH))
                    if k == 0:
                        add("v", (lambda e, sc=xbc_v(cg), s=xb_v(cg, 0, CH),
                                  w0=sv["convw"][0][:, cg:cg + 1],
                                  cb=sv["convb"][0][:, cg:cg + 1]:
                                  e.tensor_scalar(sc, s, w0, cb, ALU.mult, ALU.add)),
                            reads=xbread + K("convw") + K("convb"),
                            writes=xbc_k(cg))
                    else:
                        add("v", (lambda e, sc=xbc_v(cg), s=xb_v(cg, k, CH + k),
                                  wk=sv["convw"][0][:, k * NCG + cg:k * NCG + cg + 1]:
                                  e.scalar_tensor_tensor(sc, s, wk, sc,
                                                         ALU.mult, ALU.add)),
                            reads=xbread + K("convw") + xbc_k(cg),
                            writes=xbc_k(cg))

            for k in range(4):
                conv_sweep(k)
                wy_group()
                wy_group()

            add("q", (lambda e, d=x_dram:
                      e.dma_start(
                          out=rg1_t.ap().rearrange("p (g n) -> p g n",
                                                   g=NCG, n=CHH)[:, :, 0:CH],
                          in_=d.rearrange("(g p) n -> p g n",
                                          p=128, g=NCG)[:, :, 3:CHH])),
                writes=R1.kall(), kind="dma", dma_sem="x")

            set_pool(SCR_MAIN)

            def gate_group(wname, mc, out_emit):
                slot = slab_small(wname, mc)
                p2 = [ps_next(), ps_next()]
                mm_group(K("wsm", slot), wsm[slot], NCG,
                         lambda k, a, b: xbc_v(k, a, b),
                         lambda k, a, b: xbc_k(k, a, b), G_MAIN, p2)
                sv_, sk_ = scr_next()
                sb16 = sv_(0, 512).bitcast(BF16)
                for gi, (c0, c1) in enumerate(G_MAIN):
                    add("s", (lambda e, p=ps[p2[gi]], sc=sb16[:, c0:c1], w=c1 - c0:
                              e.activation(sc, p[:, 0:w], AF.Sigmoid)),
                        reads=KP(p2[gi]), writes=sk_(0, 512))
                out_emit(sb16, lambda: sk_(0, 512))

            def wa_out(mc):
                def go(sb16, skf):
                    add("v", (lambda e, sc=sb16, d=la_v(mc),
                              g=sv_gneg[:, mc:mc + 1]:
                              e.tensor_scalar(d, sc, g, None, ALU.mult)),
                        reads=skf() + K("gneg"), writes=la_k(mc))
                return go

            def wi_out(mc):
                def go(sb16, skf):
                    add("v", (lambda e, sc=sb16, d=ixb_v(mc), x=xbc_v(mc):
                              e.tensor_tensor(d, sc, x, ALU.mult)),
                        reads=skf() + xbc_k(mc), writes=ixb_k(mc))
                return go

            def gelu_unit(g_v, g_k):
                tAf, tAk = scr_next()
                tBf, tBk = scr_next()
                tA = tAf(0, 512).bitcast(BF16)
                tB = tBf(0, 512).bitcast(BF16)
                kA = lambda a=0, b=CH: tAk(0, 512)
                kB = lambda a=0, b=CH: tBk(0, 512)
                add("s", (lambda e, sc=tA, g=g_v: e.activation(sc, g, AF.Square)),
                    reads=g_k, writes=kA())
                add("s", (lambda e, sc=tA, sd=tB:
                          e.activation(sd, sc, AF.Identity,
                                       scale=2.0 * GC * GA, bias=2.0 * GC)),
                    reads=kA() + K("consts"), writes=kB())
                add("v", (lambda e, sc=tA, sd=tB, g=g_v:
                          e.tensor_tensor(sc, g, sd, ALU.mult)),
                    reads=g_k + kB(), writes=kA())
                add("s", (lambda e, sc=tA, sd=tB:
                          e.activation(sd, sc, AF.Sigmoid)),
                    reads=kA(), writes=kB())
                return (lambda a=0, b=CH: tA[:, a:b]), kA, (lambda a=0, b=CH: tB[:, a:b]), kB

            for mc in range(NCG):
                gate_group("wa", mc, wa_out(mc))

            def sig_half(mcs):
                for step in range(len(mcs) // 2):
                    gate_group("wi", mcs[2 * step], wi_out(mcs[2 * step]))
                    gate_group("wi", mcs[2 * step + 1], wi_out(mcs[2 * step + 1]))
                    for mc in (mcs[2 * step], mcs[2 * step + 1]):
                        ykall = yb2_k(mc) + tuple(
                            k + f":h{gi}" for gi in range(2)
                            for k in yb2_k(mc))
                        tA, tAk, tB, tBk = gelu_unit(yb2_v(mc), ykall)
                        add("v", (lambda e, g=yb2_v(mc), sd=tB():
                                  e.tensor_tensor(g, g, sd, ALU.mult)),
                            reads=ykall + tBk(), writes=ykall)

            sig_half([0, 1, 2, 3])
            set_pool(SCR_MAIN + SCR_R6)
            add("v", (lambda e: e.memset(rg6_t[:, 0:CH], 0.0)),
                reads=(), writes=R6.k(0, CH))

            def d3_pair(mcs):
                u = {}
                for mc in mcs:
                    tiles = []
                    for _ in range(3):
                        tv, tk = scr_next()
                        tiles.append(((lambda a=0, b=CH, tv=tv:
                                       tv(0, CH).bitcast(BF16)[:, a:b]), tk))
                    u[mc] = tuple(tiles)
                for mc in mcs:
                    add("v", (lambda e, l=la_v(mc), mc=mc:
                              e.reduce_sum(ends[:, mc:mc + 1], l,
                                           axis=mybir.AxisListType.X)),
                        reads=la_k(mc), writes=(f"ends:c{mc}",))
                for mc in mcs:
                    (sa, sak), _, _ = u[mc]
                    add("s", (lambda e, sc=sa(), l=la_v(mc):
                              e.activation(sc, l, AF.Exp)),
                        reads=la_k(mc), writes=sak())
                for mc in mcs:
                    (sa, sak), (sb2, sb2k), _ = u[mc]
                    add("v", (lambda e, sa_=sa(), sc=sb2():
                              e.tensor_tensor(sc, sa_, sa_, ALU.mult)),
                        reads=sak() + sb2k(), writes=sb2k())
                for mc in mcs:
                    _, (sb2, sb2k), _ = u[mc]
                    add("v", (lambda e, sc=sb2():
                              e.tensor_scalar(sc, sc, -1.0, 1.0, ALU.mult, ALU.add)),
                        reads=sb2k(), writes=sb2k())
                for mc in mcs:
                    _, (sb2, sb2k), _ = u[mc]
                    add("s", (lambda e, sc=sb2(): e.activation(sc, sc, AF.Ln)),
                        reads=sb2k(), writes=sb2k())
                for mc in mcs:
                    _, (sb2, sb2k), _ = u[mc]
                    add("s", (lambda e, sc=sb2():
                              e.activation(sc, sc, AF.Exp, scale=0.5)),
                        reads=sb2k() + K("consts"), writes=sb2k())
                for mc in mcs:
                    _, (sb2, sb2k), (sbb, sbbk) = u[mc]
                    add("v", (lambda e, sc=sb2(), sd=sbb(), i=ixb_v(mc):
                              e.scalar_tensor_tensor(sd, sc, 1e-12, i,
                                                     ALU.max, ALU.mult)),
                        reads=sb2k() + ixb_k(mc), writes=sbbk())
                for mc in mcs:
                    (sa, sak), _, (sbb, sbbk) = u[mc]
                    add("v", (lambda e, sc=sa(), sd=sbb(), d=h_v(mc):
                              e.tensor_tensor_scan(d, sc, sd, 0.0,
                                                   ALU.mult, ALU.add)),
                        reads=sak() + sbbk(), writes=h_k(mc))

            d3_pair([0, 1])
            d3_pair([2, 3])
            sig_half([4, 5, 6, 7])
            set_pool(SCR_MAIN + SCR_R6 + SCR_XC)
            d3_pair([4, 5])
            d3_pair([6, 7])

            add("s", (lambda e: e.activation(ends[:, 0:8], ends[:, 0:8], AF.Exp)),
                reads=tuple(f"ends:c{i}" for i in range(8)),
                writes=tuple(f"ends:c{i}" for i in range(8)))
            for mc in range(NCG):
                add("v", (lambda e, mc=mc, h=h_v(mc):
                          e.tensor_copy(ends[:, 8 + mc:9 + mc], h[:, CH - 1:CH])),
                    reads=h_k(mc), writes=(f"ends:c{8 + mc}",))
            pT = ps_next()
            add("pe", (lambda e, p=ps[pT]:
                       e.transpose(p[0:16, 0:128], ends[:, :], sv["ident"][0][:, :])),
                reads=tuple(f"ends:c{i}" for i in range(16)) + K("ident"),
                writes=KP(pT))
            add("s", (lambda e, p=ps[pT]: e.copy(endsT[:, :], p[0:16, 0:128])),
                reads=KP(pT), writes=K_endsT)
            add("q", (lambda e, d=ag_in[pi]: e.dma_start(out=d.ap(), in_=endsT[:, :])),
                reads=K_endsT, writes=K(f"agin{pi}"), kind="dma",
                dma_sem=f"ag{pi}")
            cop = add("p", (lambda e, i=pi:
                            e.collective_compute(
                                "AllGather", ALU.bypass,
                                replica_groups=[list(range(N_CORES))],
                                ins=[ag_in[i].ap().opt()],
                                outs=[ag_out[i].ap().opt()])),
                      reads=K(f"agin{pi}"), writes=K(f"agout{pi}"), kind="coll")
            cop.dma_sem = cc[pi]
            cop.dma_val = 1
            for mc in range(NCG):
                sa2, sa2k = scr_next()
                add("s", (lambda e, sc=sa2(), l=la_v(mc):
                          e.activation(sc, l, AF.Exp)),
                    reads=la_k(mc), writes=sa2k())
                add("v", (lambda e, sc=sa2(), d=A_v(mc):
                          e.tensor_tensor_scan(d, sc, rg6_t[:, 0:CH], 1.0,
                                               ALU.mult, ALU.add)),
                    reads=sa2k() + R6.k(0, CH), writes=A_k(mc))
            add("q", (lambda e, d=ag_out[pi]: e.dma_start(out=gT[:, :], in_=d.ap())),
                reads=K(f"agout{pi}"), writes=K_gTall, kind="dma",
                dma_sem=f"ag{pi}")
            pT2 = ps_next()
            add("pe", (lambda e, p=ps[pT2]:
                       e.transpose(p[:, 0:128], gT[:, :], sv["ident"][0][:, :])),
                reads=K_gTall + K("ident"), writes=KP(pT2))
            add("s", (lambda e, p=ps[pT2]: e.copy(gAll[:, :], p[:, 0:128])),
                reads=KP(pT2), writes=K_gTall)

            def fold(mname, dstt, dkey):
                add("v", (lambda e, m=sv[mname][0]: e.tensor_tensor(
                    fsc[0][:, :], gAll[:, :], m[:, :], ALU.mult)),
                    reads=K_gTall + K(mname), writes=K_fsc[0])
                add("v", (lambda e: e.tensor_tensor(
                    fsc[1][:, 0:64], fsc[0][:, 0:64], fsc[0][:, 64:128], ALU.add)),
                    reads=K_fsc[0], writes=K_fsc[1])
                add("v", (lambda e: e.tensor_tensor(
                    fsc[0][:, 0:32], fsc[1][:, 0:32], fsc[1][:, 32:64], ALU.add)),
                    reads=K_fsc[1], writes=K_fsc[0])
                add("v", (lambda e, d=dstt: e.tensor_tensor(
                    d[:, 0:16], fsc[0][:, 0:16], fsc[0][:, 16:32], ALU.add)),
                    reads=K_fsc[0], writes=dkey)

            if pi == 0:
                fold("mask_e", foldE, K("foldE"))
                fold("mask_o", foldO, K("foldO"))
                add("v", (lambda e: e.tensor_tensor(
                    tiny[0][:, :], foldO[:, 0:8], foldE[:, 8:16], ALU.mult)),
                    reads=K("foldO") + K("foldE"), writes=K("tiny", 0))
                add("v", (lambda e: e.tensor_tensor(
                    H2[:, :], tiny[0][:, :], foldO[:, 8:16], ALU.add)),
                    reads=K("tiny", 0) + K("foldO"), writes=K("H2"))
                add("v", (lambda e: e.tensor_scalar(
                    hin[:, :], foldE[:, 8:16], sv["slf"][0][:, 0:1], None, ALU.mult)),
                    reads=K("foldE") + K("slf"), writes=K("hin"))
            else:
                fold("mask_e", foldE, K("foldE"))
                add("v", (lambda e: e.tensor_tensor(
                    tiny[0][:, :], foldE[:, 0:8], H2[:, :], ALU.mult)),
                    reads=K("foldE") + K("H2"), writes=K("tiny", 0))
                add("v", (lambda e: e.tensor_tensor(
                    tiny[0][:, :], tiny[0][:, :], foldE[:, 8:16], ALU.add)),
                    reads=K("tiny", 0) + K("foldE"), writes=K("tiny", 0))
                add("v", (lambda e: e.tensor_scalar(
                    tiny[1][:, :], H2[:, :], sv["slf"][0][:, 1:2], None, ALU.mult)),
                    reads=K("H2") + K("slf"), writes=K("tiny", 1))
                add("v", (lambda e: e.tensor_scalar(
                    tiny[0][:, :], tiny[0][:, :], sv["slf"][0][:, 0:1], None, ALU.mult)),
                    reads=K("tiny", 0) + K("slf"), writes=K("tiny", 0))
                add("v", (lambda e: e.tensor_tensor(
                    hin[:, :], tiny[0][:, :], tiny[1][:, :], ALU.add)),
                    reads=K("tiny", 0) + K("tiny", 1), writes=K("hin"))

            for mc in range(NCG):
                add("v", (lambda e, a=A_v(mc), d=h_v(mc), s=hin[:, mc:mc + 1]:
                          e.scalar_tensor_tensor(d, a, s, d, ALU.mult, ALU.add)),
                    reads=A_k(mc) + K("hin") + h_k(mc), writes=h_k(mc))
            for mc in range(NCG):
                add("v", (lambda e, d=hyb_v(mc), h=h_v(mc), y=yb2_v(mc):
                          e.tensor_tensor(d, h, y, ALU.mult)),
                    reads=h_k(mc) + yb2_k(mc), writes=hyb_k(mc))

            set_pool(SCR_MAIN)
            rms2_st = norm_stats_begin(False)
            for mc in range(NCG):
                slot = slab_small("wo", mc)
                p2 = [ps_next(), ps_next()]
                mm_group(K("wsm", slot), wsm[slot], NCG,
                         lambda k, a, b: hyb_v(k, a, b),
                         lambda k, a, b: hyb_k(k, a, b), G_MAIN, p2)
                for gi, (c0, c1) in enumerate(G_MAIN):
                    add("v", (lambda e, p=ps[p2[gi]], x=x_v(mc, c0, c1),
                              d=h_v(mc, c0, c1), w=c1 - c0:
                              e.tensor_tensor(d, p[:, 0:w], x, ALU.add)),
                        reads=KP(p2[gi]) + x_k(mc, c0, c1),
                        writes=h_k(mc, c0, c1))
                add("v", (lambda e, d=ln1_v(mc), h=h_v(mc), x=x_v(mc, 0, CH):
                          e.tensor_tensor(d, h, x, ALU.add)),
                    reads=h_k(mc) + x_k(mc, 0, CH), writes=ln1_k(mc))
            for mc in range(NCG):
                norm_stats_cg(rms2_st, mc,
                              lambda cg, a, b: h_v(cg, a, b),
                              lambda cg, a, b: h_k(cg, a, b),
                              mc == 0, mc == NCG - 1)

            rms_finish(rms2_st,
                       lambda cg, a, b: h_v(cg, a, b),
                       lambda cg, a, b: h_k(cg, a, b),
                       sv["norm2w"][0], K("norm2w"),
                       lambda cg, a, b: u_v(cg, a, b),
                       lambda cg, a, b: u_k(cg, a, b), EPS_RMS)

            set_pool(SCR_MAIN + SCR_LA)
            for half in range(2):
                for jp in range(0, 16, 2):
                    pgs, pus = {}, {}
                    for j in (jp, jp + 1):
                        m2 = half * 16 + j
                        slot_g = slab_small("wg", m2)
                        pgs[j] = [ps_next(), ps_next()]
                        mm_group(K("wsm", slot_g), wsm[slot_g], NCG,
                                 lambda k, a, b: u_v(k, a, b),
                                 lambda k, a, b: u_k(k, a, b), G_MAIN, pgs[j])
                        slot_u = slab_small("wu", m2)
                        pus[j] = [ps_next(), ps_next()]
                        mm_group(K("wsm", slot_u), wsm[slot_u], NCG,
                                 lambda k, a, b: u_v(k, a, b),
                                 lambda k, a, b: u_k(k, a, b), G_MAIN, pus[j])
                    tl = {}
                    for j in (jp, jp + 1):
                        for gi, (c0, c1) in enumerate(G_MAIN):
                            add("s", (lambda e, p=ps[pgs[j][gi]],
                                      d=prod_v(j, c0, c1), w=c1 - c0:
                                      e.copy(d, p[:, 0:w])),
                                reads=KP(pgs[j][gi]), writes=prod_k(j, c0, c1))
                    for j in (jp, jp + 1):
                        tAf, tAk = scr_next()
                        tBf, tBk = scr_next()
                        tl[j] = (tAf(0, 512).bitcast(BF16),
                                 lambda a=0, b=CH, k=tAk: k(0, 512),
                                 tBf(0, 512).bitcast(BF16),
                                 lambda a=0, b=CH, k=tBk: k(0, 512))
                    for j in (jp, jp + 1):
                        tA, kA, tB, kB = tl[j]
                        add("s", (lambda e, sc=tA, g=prod_v(j):
                                  e.activation(sc, g, AF.Square)),
                            reads=prod_k(j), writes=kA())
                    for j in (jp, jp + 1):
                        tA, kA, tB, kB = tl[j]
                        add("s", (lambda e, sc=tA, sd=tB:
                                  e.activation(sd, sc, AF.Identity,
                                               scale=2.0 * GC * GA, bias=2.0 * GC)),
                            reads=kA() + K("consts"), writes=kB())
                    for j in (jp, jp + 1):
                        tA, kA, tB, kB = tl[j]
                        add("v", (lambda e, sc=tA, sd=tB, g=prod_v(j):
                                  e.tensor_tensor(sc, g, sd, ALU.mult)),
                            reads=prod_k(j) + kB(), writes=kA())
                    for j in (jp, jp + 1):
                        tA, kA, tB, kB = tl[j]
                        add("s", (lambda e, sc=tA, sd=tB:
                                  e.activation(sd, sc, AF.Sigmoid)),
                            reads=kA(), writes=kB())
                    for j in (jp, jp + 1):
                        tA, kA, tB, kB = tl[j]
                        add("v", (lambda e, sc=tA, sd=tB, g=prod_v(j):
                                  e.tensor_tensor(sc, g, sd, ALU.mult)),
                            reads=prod_k(j) + kB(), writes=kA())
                    for j in (jp, jp + 1):
                        tA, kA, tB, kB = tl[j]
                        for gi, (c0, c1) in enumerate(G_MAIN):
                            add("v", (lambda e, p=ps[pus[j][gi]], sc=tA[:, c0:c1],
                                      d=prod_v(j, c0, c1), w=c1 - c0:
                                      e.tensor_tensor(d, sc, p[:, 0:w], ALU.mult)),
                                reads=KP(pus[j][gi]) + kA(),
                                writes=prod_k(j, c0, c1))

                if half == 1:
                    set_pool(SCR_MAIN)
                    ln1_st = norm_stats_begin(True)
                for md in range(NCG):
                    slot = slab_big("wd", md, half)
                    p2 = [ps_next(), ps_next()]
                    mm_group(K("wbg", slot), wbg[slot], 16,
                             lambda k, a, b: prod_v(k, a, b),
                             lambda k, a, b: prod_k(k, a, b), G_MAIN, p2)
                    for gi, (c0, c1) in enumerate(G_MAIN):
                        if half == 0:
                            add("v", (lambda e, p=ps[p2[gi]],
                                      d=x_v(md, c0, c1), w=c1 - c0:
                                      e.tensor_copy(d, p[:, 0:w])),
                                reads=KP(p2[gi]), writes=x_k(md, c0, c1))
                        else:
                            add("v", (lambda e, p=ps[p2[gi]],
                                      d=x_v(md, c0, c1), w=c1 - c0:
                                      e.tensor_tensor(d, p[:, 0:w], d, ALU.add)),
                                reads=KP(p2[gi]) + x_k(md, c0, c1),
                                writes=x_k(md, c0, c1))
                    if half == 1:
                        add("v", (lambda e, d=ln1_v(md), wv=x_v(md, 0, CH):
                                  e.tensor_tensor(d, wv, d, ALU.add)),
                            reads=x_k(md, 0, CH) + ln1_k(md), writes=ln1_k(md))
            for md in range(NCG):
                norm_stats_cg(ln1_st, md,
                              lambda cg, a, b: ln1_v(cg, a, b),
                              lambda cg, a, b: ln1_k(cg, a, b),
                              md == 0, md == NCG - 1)

            ln_finish(ln1_st,
                      lambda cg, a, b: ln1_v(cg, a, b),
                      lambda cg, a, b: ln1_k(cg, a, b),
                      lambda cg, a, b: x_v(cg, a, b),
                      lambda cg, a, b: x_k(cg, a, b), EPS_LN,
                      extra=(lambda cg, a, b: u_v(cg, a, b),
                             lambda cg, a, b: u_k(cg, a, b)))

            for half in range(2):
                for j in range(16):
                    m2 = half * 16 + j
                    slot = slab_small("w1", m2)
                    p2 = [ps_next(), ps_next()]
                    mm_group(K("wsm", slot), wsm[slot], NCG,
                             lambda k, a, b: u_v(k, a, b),
                             lambda k, a, b: u_k(k, a, b), G_MAIN, p2)
                    for gi, (c0, c1) in enumerate(G_MAIN):
                        add("s", (lambda e, p=ps[p2[gi]], d=prod_v(j, c0, c1),
                                  bv=sv["b1v"][0][:, m2:m2 + 1], w=c1 - c0:
                                  e.activation(d, p[:, 0:w], AF.Relu, bias=bv)),
                            reads=KP(p2[gi]) + K("b1v"),
                            writes=prod_k(j, c0, c1))
                if half == 1:
                    set_pool(SCR_MAIN)
                    ln2_st = norm_stats_begin(True)
                for md in range(NCG):
                    slot = slab_big("w2", md, half)
                    p2 = [ps_next(), ps_next()]
                    mm_group(K("wbg", slot), wbg[slot], 16,
                             lambda k, a, b: prod_v(k, a, b),
                             lambda k, a, b: prod_k(k, a, b), G_MAIN, p2)
                    for gi, (c0, c1) in enumerate(G_MAIN):
                        if half == 0:
                            add("v", (lambda e, p=ps[p2[gi]],
                                      d=h_v(md, c0, c1), w=c1 - c0:
                                      e.tensor_copy(d, p[:, 0:w])),
                                reads=KP(p2[gi]), writes=h_k(md, c0, c1))
                        else:
                            add("v", (lambda e, p=ps[p2[gi]],
                                      d=h_v(md, c0, c1), w=c1 - c0:
                                      e.tensor_tensor(d, p[:, 0:w], d, ALU.add)),
                                reads=KP(p2[gi]) + h_k(md, c0, c1),
                                writes=h_k(md, c0, c1))
                    if half == 1:
                        add("v", (lambda e, d=h_v(md), bv=sv["b2v"][0][:, md:md + 1],
                                  x2=x_v(md, 0, CH):
                                  e.scalar_tensor_tensor(d, d, bv, x2,
                                                         ALU.add, ALU.add)),
                            reads=h_k(md) + K("b2v") + x_k(md, 0, CH),
                            writes=h_k(md))
            if head_next is not None:
                head_next()
                set_pool(SCR_MAIN)
            for md in range(NCG):
                norm_stats_cg(ln2_st, md,
                              lambda cg, a, b: h_v(cg, a, b),
                              lambda cg, a, b: h_k(cg, a, b),
                              md == 0, md == NCG - 1)

            ln_finish(ln2_st,
                      lambda cg, a, b: h_v(cg, a, b),
                      lambda cg, a, b: h_k(cg, a, b),
                      lambda cg, a, b: SCR_XC[cg % 4][0](a, b),
                      lambda cg, a, b: SCR_XC[cg % 4][1](a, b), EPS_LN,
                      stream_out=ys[pi])

        pass_head(0)
        do_pass(0, head_next=lambda: pass_head(1))
        do_pass(1)

        P.number()
        block = E(nc.Block())

        @block.tensor
        def _(t):
            P.emit_engine("pe", t)

        @block.vector
        def _(v):
            P.emit_engine("v", v)

        @block.scalar
        def _(s):
            P.emit_engine("s", s)

        @block.gpsimd
        def _(g):
            P.emit_engine("p", g)

        @block.sync
        def _(q):
            P.emit_engine("q", q)
            q.wait_ge(dsem["y0"], P.dma_counts.get("y0", 0))
            q.wait_ge(dsem["y1"], P.dma_counts.get("y1", 0))

    return nc


def _tile_w(w):
    Kd, Md = w.shape
    t = w.reshape(Kd // 128, 128, Md // 128, 128).transpose(2, 1, 0, 3)
    return np.ascontiguousarray(t.reshape(Md // 128 * 128, Kd))


def _to_fm(v):
    return np.ascontiguousarray(v.reshape(-1, 128).T)


_NC_CACHE = {}


def make_in_maps(L0_attn_x, L0_attn_norm1_w, L0_attn_Wx, L0_attn_Wy,
                 L0_attn_conv_w, L0_attn_conv_b, L0_attn_Wi, L0_attn_Wa,
                 L0_attn_lam, L0_attn_Wo, L0_attn_norm2_w, L0_attn_Wg,
                 L0_attn_Wu, L0_attn_Wd, L0_ffn_W1, L0_ffn_b1, L0_ffn_W2,
                 L0_ffn_b2):
    x = np.asarray(L0_attn_x, dtype=np.float32)

    shared = dict(
        wx=_tile_w(np.asarray(L0_attn_Wx, np.float32)),
        wy=_tile_w(np.asarray(L0_attn_Wy, np.float32)),
        wa=_tile_w(np.asarray(L0_attn_Wa, np.float32)),
        wi=_tile_w(np.asarray(L0_attn_Wi, np.float32)),
        wo=_tile_w(np.asarray(L0_attn_Wo, np.float32)),
        wg=_tile_w(np.asarray(L0_attn_Wg, np.float32)),
        wu=_tile_w(np.asarray(L0_attn_Wu, np.float32)),
        wd=_tile_w(np.asarray(L0_attn_Wd, np.float32)),
        w1=_tile_w(np.asarray(L0_ffn_W1, np.float32)),
        w2=_tile_w(np.asarray(L0_ffn_W2, np.float32)),
        norm1w=_to_fm(np.asarray(L0_attn_norm1_w, np.float32)),
        norm2w=_to_fm(np.asarray(L0_attn_norm2_w, np.float32)),
        convb=_to_fm(np.asarray(L0_attn_conv_b, np.float32)),
        convw=np.ascontiguousarray(np.concatenate(
            [_to_fm(np.asarray(L0_attn_conv_w, np.float32)[k])
             for k in range(4)], axis=1)),
        lam=_to_fm(np.asarray(L0_attn_lam, np.float32)),
        b1v=_to_fm(np.asarray(L0_ffn_b1, np.float32)),
        b2v=_to_fm(np.asarray(L0_ffn_b2, np.float32)),
        ident=np.eye(128, dtype=np.float32),
    )

    in_maps = []
    for c in range(N_CORES):
        pair, role = c // 2, c % 2
        m = dict(shared)
        for pi in range(2):
            ck = 2 * pi + role
            t0 = ck * CH
            if t0 == 0:
                sl = np.concatenate(
                    [np.zeros((3, D), np.float32), x[pair, 0:CH, :]], axis=0)
            else:
                sl = x[pair, t0 - 3:t0 + CH, :]
            m[f"x{pi}"] = np.ascontiguousarray(sl.T)
        me = np.zeros((128, 128), np.float32)
        mo = np.zeros((128, 128), np.float32)
        me[:, (2 * pair) * 16:(2 * pair + 1) * 16] = 1.0
        mo[:, (2 * pair + 1) * 16:(2 * pair + 2) * 16] = 1.0
        m["mask_e"] = me
        m["mask_o"] = mo
        sl2 = np.zeros((128, 2), np.float32)
        sl2[:, 0] = float(role)
        sl2[:, 1] = 1.0 - float(role)
        m["slf"] = sl2
        in_maps.append(m)
    return in_maps


def assemble(results):
    out = np.empty((B, SEQ, D), dtype=np.float32)
    for c in range(N_CORES):
        pair, role = c // 2, c % 2
        for pi in range(2):
            ck = 2 * pi + role
            y = results[c][f"y{pi}"]
            out[pair, ck * CH:(ck + 1) * CH, :] = y.T
    return out


def kernel(**inputs):
    in_maps = make_in_maps(**inputs)
    if "nc" not in _NC_CACHE:
        _NC_CACHE["nc"] = build_nc()
    nc = _NC_CACHE["nc"]
    res = run_bass_kernel_spmd(nc, in_maps, core_ids=list(range(N_CORES)))
    return assemble(res.results)


def _dead(results):
    out = np.empty((B, SEQ, D), dtype=np.float32)
    return out


# version 27
# speedup vs baseline: 1.0336x; 1.0336x over previous
import contextlib
import numpy as np

import concourse.bass as bass
import concourse.mybir as mybir
from concourse.bass_utils import run_bass_kernel_spmd

F32 = mybir.dt.float32
BF16 = mybir.dt.bfloat16
AF = mybir.ActivationFunctionType
ALU = mybir.AluOpType

D = 1024
HID = 4096
B = 4
SEQ = 4096
CH = 1024
CHH = CH + 3
NCG = D // 128
NH = HID // 128
N_CORES = 8
EPS_RMS = 1e-6
EPS_LN = 1e-5
GC = 0.7978845608028654
GA = 0.044715

G_HALO = [(0, 3), (3, 515), (515, 1027)]
G_MAIN = [(0, 512), (512, 1024)]


class Op:
    __slots__ = ("engine", "emit", "reads", "writes", "kind", "needs_inc",
                 "inc_val", "waits", "dma_sem", "dma_val", "dma_stream", "seq")

    def __init__(self, engine, emit, reads, writes, kind):
        self.engine = engine
        self.emit = emit
        self.reads = reads
        self.writes = writes
        self.kind = kind
        self.needs_inc = False
        self.inc_val = None
        self.waits = []
        self.dma_sem = None
        self.dma_val = None


class Plan:
    def __init__(self):
        self.ops = {e: [] for e in ("pe", "v", "s", "p", "q")}
        self.last_writer = {}
        self.readers = {}
        self.since_drain = {"v": set(), "s": set()}
        self.eng_sems = {}
        self.dma_sems = {}
        self.dma_counts = {}

    def _dep(self, op, dep):
        if dep is op:
            return
        if dep.kind == "coll":
            op.waits.append(("sem", dep.dma_sem, dep.dma_val))
            return
        if dep.kind == "dma":
            op.waits.append(("sem", self.dma_sems[dep.dma_stream],
                             self.dma_counts[dep.dma_stream]))
            return
        if dep.engine == op.engine:
            if op.engine == "pe":
                return
            if op.kind in ("dma", "coll") or op.engine == "p":
                op.waits.append(dep)
                return
            if id(dep) in self.since_drain.get(op.engine, set()):
                d = Op(op.engine, None, (), (), "drain")
                self.ops[op.engine].append(d)
                self.since_drain[op.engine] = set()
            return
        op.waits.append(dep)

    def add(self, engine, emit, reads=(), writes=(), kind="compute",
            dma_sem=None):
        op = Op(engine, emit, tuple(reads), tuple(writes), kind)
        for k in op.reads:
            lw = self.last_writer.get(k)
            if lw is not None:
                self._dep(op, lw)
        for k in op.writes:
            lw = self.last_writer.get(k)
            if lw is not None:
                self._dep(op, lw)
            for r in self.readers.get(k, ()):
                self._dep(op, r)
        if kind == "dma":
            op.dma_stream = dma_sem
            self.dma_counts[dma_sem] = self.dma_counts.get(dma_sem, 0) + 16
            op.dma_sem = self.dma_sems[dma_sem]
            op.dma_val = self.dma_counts[dma_sem]
        op.seq = len(self.ops[engine])
        self.ops[engine].append(op)
        for k in op.writes:
            self.last_writer[k] = op
            self.readers[k] = []
        if engine in ("v", "s") and kind == "compute":
            self.since_drain[engine].add(id(op))
        for k in op.reads:
            self.readers.setdefault(k, []).append(op)
        return op

    def number(self):
        for e in self.ops:
            for op in self.ops[e]:
                best = {}
                rest = []
                for dep in op.waits:
                    if isinstance(dep, tuple):
                        rest.append(dep)
                        continue
                    b = best.get(dep.engine)
                    if b is None or dep.seq > b.seq:
                        best[dep.engine] = dep
                for dep in best.values():
                    dep.needs_inc = True
                op.waits = rest + list(best.values())
        for e in self.ops:
            cnt = 0
            for op in self.ops[e]:
                if op.needs_inc and op.kind == "compute":
                    cnt += 1
                    op.inc_val = cnt

    def emit_engine(self, ename, eng):
        hwm = {}
        for op in self.ops[ename]:
            if op.kind == "drain":
                eng.drain()
                continue
            for dep in op.waits:
                if isinstance(dep, tuple):
                    _, sem, val = dep
                else:
                    sem, val = self.eng_sems[dep.engine], dep.inc_val
                    assert val is not None
                if hwm.get(id(sem), -1) < val:
                    eng.wait_ge(sem, val)
                    hwm[id(sem)] = val
            inst = op.emit(eng)
            if op.kind == "dma":
                inst.then_inc(op.dma_sem, 16)
            elif op.kind == "coll":
                inst.then_inc(op.dma_sem, op.dma_val)
            elif op.needs_inc:
                inst.then_inc(self.eng_sems[op.engine], 1)


class Region:
    PAGE = 512

    def __init__(self, tensor, name, n_elems):
        self.t = tensor
        self.name = name
        self.n = n_elems

    def v(self, c0, c1):
        return self.t[:, c0:c1]

    def k(self, c0, c1):
        return tuple(f"{self.name}:{i}"
                     for i in range(c0 // self.PAGE, (c1 + self.PAGE - 1) // self.PAGE))

    def kall(self):
        return self.k(0, self.n)


def build_nc():
    nc = bass.Bass("TRN2", target_bir_lowering=False)

    def din(name, shape, dt=F32):
        return nc.dram_tensor(name, shape, dt, kind="ExternalInput").ap()

    xs = [din("x0", [D, CHH]), din("x1", [D, CHH])]
    wdr = {n: din(n, [NCG * 128, D]) for n in ("wx", "wy", "wa", "wi", "wo")}
    wdr.update({n: din(n, [NH * 128, D]) for n in ("wg", "wu", "w1")})
    wdr.update({n: din(n, [NCG * 128, HID]) for n in ("wd", "w2")})
    norm1w = din("norm1w", [128, NCG])
    norm2w = din("norm2w", [128, NCG])
    convb = din("convb", [128, NCG])
    convw = din("convw", [128, 4 * NCG])
    lam = din("lam", [128, NCG])
    b1v = din("b1v", [128, NH])
    b2v = din("b2v", [128, NCG])
    mask_e = din("mask_e", [128, 128])
    mask_o = din("mask_o", [128, 128])
    slf = din("slf", [128, 2])
    ident = din("ident", [128, 128])

    ys = [nc.dram_tensor(f"y{i}", [D, CH], F32, kind="ExternalOutput").ap()
          for i in range(2)]
    ag_in = [nc.dram_tensor(f"ag{i}_in", [16, 128], F32) for i in range(2)]
    ag_out = [nc.dram_tensor(f"ag{i}_out", [128, 128], F32) for i in range(2)]

    with contextlib.ExitStack() as ctx:
        E = ctx.enter_context

        rg1_t = E(nc.sbuf_tensor("rg1", [128, NCG * CHH], F32))
        rg2_t = E(nc.sbuf_tensor("rg2", [128, NCG * CH], F32))
        rg3_t = E(nc.sbuf_tensor("rg3", [128, NCG * CH], F32))
        rg5_t = E(nc.sbuf_tensor("rg5", [128, NCG * CHH + 3 * NCG * CH], BF16))
        rg6_t = E(nc.sbuf_tensor("rg6", [128, NCG * CHH], BF16))
        R1 = Region(rg1_t, "rg1", NCG * CHH)
        R2 = Region(rg2_t, "rg2", NCG * CH)
        R3 = Region(rg3_t, "rg3", NCG * CH)
        R5 = Region(rg5_t, "rg5", NCG * CHH + 3 * NCG * CH)
        R5f = Region(rg5_t.ap().bitcast(F32), "rg5", (NCG * CHH + 3 * NCG * CH) // 2)
        R5f.PAGE = 256
        R3b = Region(rg3_t.ap().bitcast(BF16), "rg3", 2 * NCG * CH)
        R3b.PAGE = 1024
        R6 = Region(rg6_t, "rg6", NCG * CHH)

        scr_t = [E(nc.sbuf_tensor(f"scr{i}", [128, CH], F32)) for i in range(3)]
        wsm = [E(nc.sbuf_tensor(f"wsm{i}", [128, D], BF16)) for i in range(3)]
        wbg = [E(nc.sbuf_tensor(f"wbg{i}", [128, 2048], BF16)) for i in range(2)]

        sv = {}
        for name, dram, wdt in [
            ("norm1w", norm1w, NCG), ("norm2w", norm2w, NCG),
            ("convb", convb, NCG), ("convw", convw, 4 * NCG),
            ("lam", lam, NCG), ("b1v", b1v, NH), ("b2v", b2v, NCG),
            ("mask_e", mask_e, 128), ("mask_o", mask_o, 128),
            ("slf", slf, 2), ("ident", ident, 128),
        ]:
            sv[name] = (E(nc.sbuf_tensor("sv_" + name, [128, wdt], F32)), dram)
        sv_gneg = E(nc.sbuf_tensor("sv_gneg", [128, NCG], F32))
        CONST_VALS = [1.0, -1.0, 0.5, -0.5, 2.0, EPS_RMS, EPS_LN,
                      2.0 * GC, 2.0 * GC * GA]
        constv = E(nc.sbuf_tensor("constv", [128, len(CONST_VALS)], F32))
        for _i, _val in enumerate(CONST_VALS):
            nc.const_aps.register(F32, _val, constv[:, _i:_i + 1])
        inv1024 = E(nc.sbuf_tensor("inv1024", [128, 1], F32))
        inv1024b = E(nc.sbuf_tensor("inv1024b", [128, 1], BF16))
        ones_row = E(nc.sbuf_tensor("ones_row", [1, 128], F32))
        ends = E(nc.sbuf_tensor("ends", [128, 16], F32))
        foldE = E(nc.sbuf_tensor("foldE", [128, 16], F32))
        foldO = E(nc.sbuf_tensor("foldO", [128, 16], F32))
        H2 = E(nc.sbuf_tensor("H2", [128, NCG], F32))
        hin = E(nc.sbuf_tensor("hin", [128, NCG], F32))
        tiny = [E(nc.sbuf_tensor(f"tiny{i}", [128, NCG], F32)) for i in range(2)]

        ps = [E(nc.psum_tensor(f"ps{i}", [128, 512], F32)) for i in range(8)]

        sem_pe = E(nc.semaphore("sem_pe"))
        sem_v = E(nc.semaphore("sem_v"))
        sem_s = E(nc.semaphore("sem_s"))
        sem_p = E(nc.semaphore("sem_p"))
        DSTREAMS = (["init", "x", "ag0", "ag1", "y0", "y1"]
                    + [f"wsm{i}" for i in range(3)]
                    + [f"wbg{i}" for i in range(2)])
        dsem = {n: E(nc.semaphore("d_" + n)) for n in DSTREAMS}
        cc = [E(nc.semaphore(f"cc{i}")) for i in range(2)]

        P = Plan()
        P.eng_sems = {"pe": sem_pe, "v": sem_v, "s": sem_s, "p": sem_p}
        P.dma_sems = dsem
        add = P.add

        def x_v(cg, a, b):
            return R1.v(cg * CHH + a, cg * CHH + b)

        def x_k(cg, a, b):
            return R1.k(cg * CHH + a, cg * CHH + b)

        def h_v(cg, a=0, b=CH):
            return R2.v(cg * CH + a, cg * CH + b)

        def h_k(cg, a=0, b=CH):
            return R2.k(cg * CH + a, cg * CH + b)

        def ln1_v(cg, a=0, b=CH):
            return R3.v(cg * CH + a, cg * CH + b)

        def ln1_k(cg, a=0, b=CH):
            return R3.k(cg * CH + a, cg * CH + b)

        def A_v(cg):
            return R3b.v(cg * CH, (cg + 1) * CH)

        def A_k(cg):
            return R3b.k(cg * CH, (cg + 1) * CH)

        OXB = 0
        OXBC = NCG * CHH
        OLA = OXBC + NCG * CH
        OIXB = OLA + NCG * CH
        ROWB = OLA // 2
        ROWB2 = ROWB + 1028
        ROWT = ROWB2 + 1024

        def rowb_v(a, b):
            return R5f.v(ROWB + a, ROWB + b)[0:1, :]

        def rowb_k(a, b):
            return R5f.k(ROWB + a, ROWB + b)

        def rowb2_v(a, b):
            return R5f.v(ROWB2 + a, ROWB2 + b)[0:1, :]

        def rowb2_k(a, b):
            return R5f.k(ROWB2 + a, ROWB2 + b)

        def rowt_v(a, b):
            return R5f.v(ROWT + a, ROWT + b)[0:1, :]

        def rowt_k(a, b):
            return R5f.k(ROWT + a, ROWT + b)

        def xb_v(cg, a, b):
            return R5.v(OXB + cg * CHH + a, OXB + cg * CHH + b)

        def xb_k(cg, a, b):
            return R5.k(OXB + cg * CHH + a, OXB + cg * CHH + b)

        def hyb_v(cg, a=0, b=CH):
            return R5.v(OXBC + cg * CH + a, OXBC + cg * CH + b)

        def hyb_k(cg, a=0, b=CH):
            return R5.k(OXBC + cg * CH + a, OXBC + cg * CH + b)

        def xbc_v(cg, a=0, b=CH):
            return R5.v(OXBC + cg * CH + a, OXBC + cg * CH + b)

        def xbc_k(cg, a=0, b=CH):
            return R5.k(OXBC + cg * CH + a, OXBC + cg * CH + b)

        def la_v(cg):
            return R5.v(OLA + cg * CH, OLA + (cg + 1) * CH)

        def la_k(cg):
            return R5.k(OLA + cg * CH, OLA + (cg + 1) * CH)

        def ixb_v(cg):
            return R5.v(OIXB + cg * CH, OIXB + (cg + 1) * CH)

        def ixb_k(cg):
            return R5.k(OIXB + cg * CH, OIXB + (cg + 1) * CH)

        def yb2_v(cg):
            return R5.v(cg * CH, (cg + 1) * CH)

        def yb2_k(cg):
            return R5.k(cg * CH, (cg + 1) * CH)

        def _pj(j):
            return j * CH if j < 8 else OXBC + (j - 8) * CH

        def prod_v(j, a=0, b=CH):
            return R5.v(_pj(j) + a, _pj(j) + b)

        def prod_k(j, a=0, b=CH):
            return R5.k(_pj(j) + a, _pj(j) + b)

        def t_v(cg, a, b):
            return R6.v(cg * CHH + a, cg * CHH + b)

        def t_k(cg, a, b):
            return R6.k(cg * CHH + a, cg * CHH + b)

        def u_v(cg, a=0, b=CH):
            return R6.v(cg * CH + a, cg * CH + b)

        def u_k(cg, a=0, b=CH):
            return R6.k(cg * CH + a, cg * CH + b)

        def K(name, i=0):
            return (f"{name}:{i}",)

        def KP(i):
            return (f"psum:{i}",)

        for name, (sb, dram) in sv.items():
            add("q", (lambda e, d=dram, s=sb: e.dma_start(out=s[:, :], in_=d)),
                writes=K(name), kind="dma", dma_sem="init")
        for _i, _val in enumerate(CONST_VALS):
            add("v", (lambda e, i=_i, val=_val:
                      e.memset(constv[:, i:i + 1], val)), writes=K("consts"))
        add("v", lambda e: e.memset(inv1024[:, :], 1.0 / 1024.0),
            writes=K("inv1024"))
        add("v", lambda e: e.memset(inv1024b[:, :], 1.0 / 1024.0),
            writes=K("inv1024b"))
        add("v", lambda e: e.memset(ones_row[:, :], 1.0), writes=K("ones_row"))

        add("s", lambda e: e.activation(tiny[0][:, :], sv["lam"][0][:, :], AF.Abs),
            reads=K("lam"), writes=K("tiny", 0))
        add("s", lambda e: e.activation(tiny[0][:, :], tiny[0][:, :], AF.Exp, scale=-1.0),
            reads=K("tiny", 0) + K("consts"), writes=K("tiny", 0))
        add("s", lambda e: e.activation(tiny[0][:, :], tiny[0][:, :], AF.Ln, bias=1.0),
            reads=K("tiny", 0) + K("consts"), writes=K("tiny", 0))
        add("s", lambda e: e.activation(tiny[1][:, :], sv["lam"][0][:, :], AF.Relu),
            reads=K("lam"), writes=K("tiny", 1))
        add("v", lambda e: e.tensor_tensor(tiny[0][:, :], tiny[0][:, :], tiny[1][:, :], ALU.add),
            reads=K("tiny", 0) + K("tiny", 1), writes=K("tiny", 0))
        add("v", lambda e: e.tensor_scalar(sv_gneg[:, :], tiny[0][:, :], -8.0, None, ALU.mult),
            reads=K("tiny", 0), writes=K("gneg"))

        st = {"sm": 0, "bg": 0, "ps": 0, "sc": 0}

        def slab_small(wname, mc):
            slot = st["sm"] % len(wsm)
            st["sm"] += 1
            d = wdr[wname]
            add("p", (lambda e, d=d, b=wsm[slot], m=mc:
                      e.dma_start(out=b[:, :], in_=d[m * 128:(m + 1) * 128, :])),
                writes=K("wsm", slot), kind="dma", dma_sem=f"wsm{slot}")
            return slot

        def slab_big(wname, mc, half):
            slot = st["bg"] % len(wbg)
            st["bg"] += 1
            d = wdr[wname]
            add("p", (lambda e, d=d, b=wbg[slot], m=mc, h=half:
                      e.dma_start(out=b[:, :],
                                  in_=d[m * 128:(m + 1) * 128, h * 2048:(h + 1) * 2048])),
                writes=K("wbg", slot), kind="dma", dma_sem=f"wbg{slot}")
            return slot

        st["resv"] = set()

        def ps_next():
            while True:
                i = st["ps"] % 8
                st["ps"] += 1
                if i not in st["resv"]:
                    return i

        def ps_reserve():
            i = ps_next()
            st["resv"].add(i)
            return i

        def ps_release(i):
            st["resv"].discard(i)

        def _mk_scr(i):
            return (lambda a=0, b=CH: scr_t[i][:, a:b],
                    lambda a=0, b=CH: (f"scr{i}:{a // 512}",)
                    if b - a <= 512 else (f"scr{i}:0", f"scr{i}:1"))

        XCF = OXBC // 2
        LAF = OLA // 2
        IXF = OIXB // 2

        def _mk_slot(base, i):
            return (lambda a=0, b=CH: R5f.v(base + i * CH + a, base + i * CH + b),
                    lambda a=0, b=CH: R5f.k(base + i * CH + a, base + i * CH + b))

        SCR_MAIN = [_mk_scr(i) for i in range(3)]

        rg6f = rg6_t.ap().bitcast(F32)

        def _mk_r6(i):
            base = 512 + i * CH
            return (lambda a=0, b=CH: rg6f[:, base + a:base + b],
                    lambda a=0, b=CH: R6.k(2 * (base + a), 2 * (base + b)))
        SCR_R6 = [_mk_r6(i) for i in range(3)]
        SCR_XC = [_mk_slot(XCF, i) for i in range(4)]
        SCR_LA = [_mk_slot(LAF, i) for i in range(4)]
        SCR_IX = [_mk_slot(IXF, i) for i in range(4)]

        endsT = scr_t[0][0:16, 0:128]
        K_endsT = ("scr0:0",)
        gT = scr_t[2][:, 0:128]
        gAll = scr_t[2][:, 0:128]
        K_gTall = ("scr2:0",)
        fsc = [scr_t[0][:, 0:128], scr_t[1][:, 0:128]]
        K_fsc = [("scr0:0",), ("scr1:0",)]

        pool_state = {"pool": SCR_MAIN, "i": 0}

        def set_pool(p):
            pool_state["pool"] = p
            pool_state["i"] = 0

        def scr_next():
            p = pool_state["pool"]
            i = pool_state["i"] % len(p)
            pool_state["i"] += 1
            return p[i]

        def mm_group(wkey, wbuf, nk, rhs_v, rhs_k, groups, pss):
            for ki in range(nk):
                lhs = wbuf[:, ki * 128:(ki + 1) * 128]
                for gi, (c0, c1) in enumerate(groups):
                    pi = pss[gi]
                    add("pe",
                        (lambda e, p=ps[pi], l=lhs, r=rhs_v(ki, c0, c1),
                         w=c1 - c0, stt=(ki == 0), sp=(ki == nk - 1):
                         e.matmul(p[:, 0:w], l, r, start=stt, stop=sp)),
                        reads=wkey + rhs_k(ki, c0, c1),
                        writes=KP(pi))

        def norm_stats_begin(want_mean):
            stn = {"sq": [ps_reserve() for _ in G_MAIN]}
            if want_mean:
                stn["mn"] = [ps_reserve() for _ in G_MAIN]
            return stn

        def norm_stats_cg(stn, cg, src_v, src_k, first, last):
            for gi, (c0, c1) in enumerate(G_MAIN):
                w = c1 - c0
                sv_, sk_ = scr_next()
                sqb = sv_(0, w).bitcast(BF16)[:, 0:w]
                add("s", (lambda e, sc=sqb, sr=src_v(cg, c0, c1):
                          e.activation(sc, sr, AF.Square)),
                    reads=src_k(cg, c0, c1), writes=sk_(0, w))
                add("pe", (lambda e, p=ps[stn["sq"][gi]], sc=sqb,
                           stt=first, sp=last, w=w:
                           e.matmul(p[0:1, 0:w], inv1024b[:, :], sc,
                                    start=stt, stop=sp)),
                    reads=sk_(0, w) + K("inv1024b"), writes=KP(stn["sq"][gi]))
                if "mn" in stn:
                    add("pe", (lambda e, p=ps[stn["mn"][gi]],
                               sr=src_v(cg, c0, c1), stt=first, sp=last, w=w:
                               e.matmul(p[0:1, 0:w], inv1024[:, :], sr,
                                        start=stt, stop=sp)),
                        reads=src_k(cg, c0, c1) + K("inv1024"),
                        writes=KP(stn["mn"][gi]))

        def rms_finish(stn, src_v, src_k, wvec, wkey, dst_v, dst_k, eps):
            for gi, (c0, c1) in enumerate(G_MAIN):
                w = c1 - c0
                pstat = stn["sq"][gi]
                add("s", (lambda e, p=ps[pstat], r=rowb_v(c0, c1), w=w:
                          e.activation(r, p[0:1, 0:w], AF.Ln, bias=eps)),
                    reads=KP(pstat) + K("consts"), writes=rowb_k(c0, c1))
                add("s", (lambda e, r=rowb_v(c0, c1):
                          e.activation(r, r, AF.Exp, scale=-0.5)),
                    reads=rowb_k(c0, c1) + K("consts"), writes=rowb_k(c0, c1))
                pbc = ps_next()
                add("pe", (lambda e, p=ps[pbc], r=rowb_v(c0, c1), w=w:
                           e.matmul(p[:, 0:w], ones_row[:, :],
                                    r, start=True, stop=True)),
                    reads=rowb_k(c0, c1) + K("ones_row"), writes=KP(pbc))
                for cg in range(NCG):
                    add("v", (lambda e, p=ps[pbc], sr=src_v(cg, c0, c1),
                              d=dst_v(cg, c0, c1), wv=wvec[:, cg:cg + 1], w=w:
                              e.scalar_tensor_tensor(d, sr, wv, p[:, 0:w],
                                                     ALU.mult, ALU.mult)),
                        reads=src_k(cg, c0, c1) + KP(pbc) + wkey,
                        writes=dst_k(cg, c0, c1))
                ps_release(pstat)

        def ln_finish(stn, src_v, src_k, dst_v, dst_k, eps, extra=None,
                      stream_out=None):
            bcs = []
            for gi, (c0, c1) in enumerate(G_MAIN):
                w = c1 - c0
                psq = stn["sq"][gi]
                psm = stn["mn"][gi]
                add("s", (lambda e, p=ps[psm], r=rowt_v(c0, c0 + w), w=w:
                          e.activation(r, p[0:1, 0:w], AF.Square)),
                    reads=KP(psm), writes=rowt_k(c0, c1))
                add("s", (lambda e, p=ps[psm], r=rowb2_v(c0, c1), w=w:
                          e.copy(r, p[0:1, 0:w])),
                    reads=KP(psm), writes=rowb2_k(c0, c1))
                add("v", (lambda e, p=ps[psq], r=rowb_v(c0, c1),
                          rt=rowt_v(c0, c1), w=w:
                          e.tensor_tensor(r, p[0:1, 0:w], rt, ALU.subtract)),
                    reads=KP(psq) + rowt_k(c0, c1), writes=rowb_k(c0, c1))
                add("s", (lambda e, r=rowb_v(c0, c1):
                          e.activation(r, r, AF.Ln, bias=eps)),
                    reads=rowb_k(c0, c1) + K("consts"), writes=rowb_k(c0, c1))
                add("s", (lambda e, r=rowb_v(c0, c1):
                          e.activation(r, r, AF.Exp, scale=-0.5)),
                    reads=rowb_k(c0, c1) + K("consts"), writes=rowb_k(c0, c1))
                add("v", (lambda e, r=rowb_v(c0, c1), r2=rowb2_v(c0, c1):
                          e.tensor_tensor(r2, r2, r, ALU.mult)),
                    reads=rowb_k(c0, c1) + rowb2_k(c0, c1), writes=rowb2_k(c0, c1))
                pb1 = ps_next()
                pb2 = ps_next()
                add("pe", (lambda e, p=ps[pb1], r=rowb_v(c0, c1), w=w:
                           e.matmul(p[:, 0:w], ones_row[:, :], r,
                                    start=True, stop=True)),
                    reads=rowb_k(c0, c1) + K("ones_row"), writes=KP(pb1))
                add("pe", (lambda e, p=ps[pb2], r=rowb2_v(c0, c1), w=w:
                           e.matmul(p[:, 0:w], ones_row[:, :], r,
                                    start=True, stop=True)),
                    reads=rowb2_k(c0, c1) + K("ones_row"), writes=KP(pb2))
                bcs.append((pb1, pb2))
                ps_release(psq)
                ps_release(psm)
            sweep = NCG if stream_out is None else 4
            for base in range(0, NCG, sweep):
                cgs = list(range(base, base + sweep))
                for gi, (c0, c1) in enumerate(G_MAIN):
                    pb1, _ = bcs[gi]
                    w = c1 - c0
                    for cg in cgs:
                        add("v", (lambda e, p=ps[pb1], sr=src_v(cg, c0, c1),
                                  d=dst_v(cg, c0, c1), w=w:
                                  e.tensor_tensor(d, sr, p[:, 0:w], ALU.mult)),
                            reads=src_k(cg, c0, c1) + KP(pb1),
                            writes=dst_k(cg, c0, c1))
                for gi, (c0, c1) in enumerate(G_MAIN):
                    _, pb2 = bcs[gi]
                    w = c1 - c0
                    for cg in cgs:
                        add("v", (lambda e, p=ps[pb2], d=dst_v(cg, c0, c1), w=w:
                                  e.tensor_tensor(d, d, p[:, 0:w], ALU.subtract)),
                            reads=dst_k(cg, c0, c1) + KP(pb2),
                            writes=dst_k(cg, c0, c1))
                for cg in cgs:
                    if extra is not None:
                        ev, ek = extra
                        for (c0, c1) in G_MAIN:
                            add("s", (lambda e, d=dst_v(cg, c0, c1),
                                      g=ev(cg, c0, c1):
                                      e.copy(g, d)),
                                reads=dst_k(cg, c0, c1), writes=ek(cg, c0, c1))
                    if stream_out is not None:
                        add("q", (lambda e, cg=cg, d=stream_out:
                                  e.dma_start(out=d[cg * 128:(cg + 1) * 128, :],
                                              in_=dst_v(cg, 0, CH))),
                            reads=dst_k(cg, 0, CH), kind="dma",
                            dma_sem=f"y{cg % 2}")
        def rms_norm(src_v, src_k, wvec, wkey, dst_v, dst_k, groups, eps):
            for (c0, c1) in groups:
                w = c1 - c0
                pstat = ps_next()
                for cg in range(NCG):
                    sv_, sk_ = scr_next()
                    sqb = sv_(0, w).bitcast(BF16)[:, 0:w]
                    add("s", (lambda e, sc=sqb, s=src_v(cg, c0, c1):
                              e.activation(sc, s, AF.Square)),
                        reads=src_k(cg, c0, c1), writes=sk_(0, w))
                    add("pe", (lambda e, p=ps[pstat], sc=sqb,
                               stt=(cg == 0), sp=(cg == NCG - 1), w=w:
                               e.matmul(p[0:1, 0:w], inv1024b[:, :], sc,
                                        start=stt, stop=sp)),
                        reads=sk_(0, w) + K("inv1024b"), writes=KP(pstat))
                add("s", (lambda e, p=ps[pstat], r=rowb_v(c0, c1), w=w:
                          e.activation(r, p[0:1, 0:w], AF.Ln, bias=eps)),
                    reads=KP(pstat) + K("consts"), writes=rowb_k(c0, c1))
                add("s", (lambda e, r=rowb_v(c0, c1):
                          e.activation(r, r, AF.Exp, scale=-0.5)),
                    reads=rowb_k(c0, c1) + K("consts"), writes=rowb_k(c0, c1))
                pbc = ps_next()
                add("pe", (lambda e, p=ps[pbc], r=rowb_v(c0, c1), w=w:
                           e.matmul(p[:, 0:w], ones_row[:, :],
                                    r, start=True, stop=True)),
                    reads=rowb_k(c0, c1) + K("ones_row"), writes=KP(pbc))
                for cg in range(NCG):
                    add("v", (lambda e, p=ps[pbc], s=src_v(cg, c0, c1),
                              d=dst_v(cg, c0, c1), wv=wvec[:, cg:cg + 1], w=w:
                              e.scalar_tensor_tensor(d, s, wv, p[:, 0:w],
                                                     ALU.mult, ALU.mult)),
                        reads=src_k(cg, c0, c1) + KP(pbc) + wkey,
                        writes=dst_k(cg, c0, c1))

        def layer_norm(src_v, src_k, dst_v, dst_k, eps, extra=None,
                       stream_out=None, src2_v=None, src2_k=None):
            if src2_v is not None:
                src_v, src_k = src2_v, src2_k
            bcs = []
            for (c0, c1) in G_MAIN:
                w = c1 - c0
                psq = ps_next()
                psm = ps_next()
                for cg in range(NCG):
                    sv_, sk_ = scr_next()
                    sqb = sv_(0, w).bitcast(BF16)[:, 0:w]
                    add("s", (lambda e, sc=sqb, s=src_v(cg, c0, c1):
                              e.activation(sc, s, AF.Square)),
                        reads=src_k(cg, c0, c1), writes=sk_(0, w))
                    add("pe", (lambda e, p=ps[psq], sc=sqb,
                               stt=(cg == 0), sp=(cg == NCG - 1), w=w:
                               e.matmul(p[0:1, 0:w], inv1024b[:, :], sc,
                                        start=stt, stop=sp)),
                        reads=sk_(0, w) + K("inv1024b"), writes=KP(psq))
                    add("pe", (lambda e, p=ps[psm], s=src_v(cg, c0, c1),
                               stt=(cg == 0), sp=(cg == NCG - 1), w=w:
                               e.matmul(p[0:1, 0:w], inv1024[:, :], s,
                                        start=stt, stop=sp)),
                        reads=src_k(cg, c0, c1) + K("inv1024"), writes=KP(psm))
                add("s", (lambda e, p=ps[psm], r=rowt_v(c0, c0 + w), w=w:
                          e.activation(r, p[0:1, 0:w], AF.Square)),
                    reads=KP(psm), writes=rowt_k(c0, c1))
                add("s", (lambda e, p=ps[psm], r=rowb2_v(c0, c1), w=w:
                          e.copy(r, p[0:1, 0:w])),
                    reads=KP(psm), writes=rowb2_k(c0, c1))
                add("v", (lambda e, p=ps[psq], r=rowb_v(c0, c1),
                          rt=rowt_v(c0, c1), w=w:
                          e.tensor_tensor(r, p[0:1, 0:w], rt, ALU.subtract)),
                    reads=KP(psq) + rowt_k(c0, c1), writes=rowb_k(c0, c1))
                add("s", (lambda e, r=rowb_v(c0, c1):
                          e.activation(r, r, AF.Ln, bias=eps)),
                    reads=rowb_k(c0, c1) + K("consts"), writes=rowb_k(c0, c1))
                add("s", (lambda e, r=rowb_v(c0, c1):
                          e.activation(r, r, AF.Exp, scale=-0.5)),
                    reads=rowb_k(c0, c1) + K("consts"), writes=rowb_k(c0, c1))
                add("v", (lambda e, r=rowb_v(c0, c1), r2=rowb2_v(c0, c1):
                          e.tensor_tensor(r2, r2, r, ALU.mult)),
                    reads=rowb_k(c0, c1) + rowb2_k(c0, c1), writes=rowb2_k(c0, c1))
                pb1 = ps_next()
                pb2 = ps_next()
                add("pe", (lambda e, p=ps[pb1], r=rowb_v(c0, c1), w=w:
                           e.matmul(p[:, 0:w], ones_row[:, :], r,
                                    start=True, stop=True)),
                    reads=rowb_k(c0, c1) + K("ones_row"), writes=KP(pb1))
                add("pe", (lambda e, p=ps[pb2], r=rowb2_v(c0, c1), w=w:
                           e.matmul(p[:, 0:w], ones_row[:, :], r,
                                    start=True, stop=True)),
                    reads=rowb2_k(c0, c1) + K("ones_row"), writes=KP(pb2))
                bcs.append((pb1, pb2))
            sweep = NCG if stream_out is None else 4
            for base in range(0, NCG, sweep):
                cgs = list(range(base, base + sweep))
                for gi, (c0, c1) in enumerate(G_MAIN):
                    pb1, _ = bcs[gi]
                    w = c1 - c0
                    for cg in cgs:
                        add("v", (lambda e, p=ps[pb1], s=src_v(cg, c0, c1),
                                  d=dst_v(cg, c0, c1), w=w:
                                  e.tensor_tensor(d, s, p[:, 0:w], ALU.mult)),
                            reads=src_k(cg, c0, c1) + KP(pb1),
                            writes=dst_k(cg, c0, c1))
                for gi, (c0, c1) in enumerate(G_MAIN):
                    _, pb2 = bcs[gi]
                    w = c1 - c0
                    for cg in cgs:
                        add("v", (lambda e, p=ps[pb2], d=dst_v(cg, c0, c1), w=w:
                                  e.tensor_tensor(d, d, p[:, 0:w], ALU.subtract)),
                            reads=dst_k(cg, c0, c1) + KP(pb2),
                            writes=dst_k(cg, c0, c1))
                for cg in cgs:
                    if extra is not None:
                        ev, ek = extra
                        for (c0, c1) in G_MAIN:
                            add("s", (lambda e, d=dst_v(cg, c0, c1),
                                      g=ev(cg, c0, c1):
                                      e.copy(g, d)),
                                reads=dst_k(cg, c0, c1), writes=ek(cg, c0, c1))
                    if stream_out is not None:
                        add("q", (lambda e, cg=cg, d=stream_out:
                                  e.dma_start(out=d[cg * 128:(cg + 1) * 128, :],
                                              in_=dst_v(cg, 0, CH))),
                            reads=dst_k(cg, 0, CH), kind="dma",
                            dma_sem=f"y{cg % 2}")

        def pass_head(pi):
            x_dram = xs[pi]
            add("q", (lambda e, d=x_dram:
                      e.dma_start(
                          out=rg1_t.ap().rearrange("p (g n) -> p g n", g=NCG, n=CHH),
                          in_=d.rearrange("(g p) n -> p g n", p=128, g=NCG))),
                writes=R1.kall(), kind="dma", dma_sem="x")
            set_pool(SCR_MAIN)
            rms_norm(x_v, x_k, sv["norm1w"][0], K("norm1w"),
                     t_v, t_k, G_HALO, EPS_RMS)

        def do_pass(pi, head_next=None):
            x_dram = xs[pi]

            for mc in range(NCG):
                slot = slab_small("wx", mc)
                p3 = [ps_next(), ps_next(), ps_next()]
                mm_group(K("wsm", slot), wsm[slot], NCG, t_v, t_k, G_HALO, p3)
                for gi, (c0, c1) in enumerate(G_HALO):
                    add("s", (lambda e, p=ps[p3[gi]], d=xb_v(mc, c0, c1), w=c1 - c0:
                              e.copy(d, p[:, 0:w])),
                        reads=KP(p3[gi]),
                        writes=tuple(k + f":g{gi}" for k in xb_k(mc, c0, c1)))

            accs = SCR_LA + SCR_IX
            wy_state = {"m": 0}

            def wy_group():
                mc = wy_state["m"]
                wy_state["m"] += 1
                slot = slab_small("wy", mc)
                p2 = [ps_next(), ps_next()]
                mm_group(K("wsm", slot), wsm[slot], NCG, t_v, t_k,
                         [(3, 515), (515, 1027)], p2)
                for gi, (c0, c1) in enumerate(G_MAIN):
                    add("s", (lambda e, p=ps[p2[gi]], d=yb2_v(mc)[:, c0:c1],
                              w=c1 - c0:
                              e.copy(d, p[:, 0:w])),
                        reads=KP(p2[gi]), writes=yb2_k(mc))

            def conv_sweep(k):
                for cg in range(NCG):
                    xbread = xb_k(cg, 0, CHH) + tuple(
                        k + f":g{gi}" for gi in range(3)
                        for k in xb_k(cg, 0, CHH))
                    if k == 0:
                        add("v", (lambda e, sc=xbc_v(cg), s=xb_v(cg, 0, CH),
                                  w0=sv["convw"][0][:, cg:cg + 1],
                                  cb=sv["convb"][0][:, cg:cg + 1]:
                                  e.tensor_scalar(sc, s, w0, cb, ALU.mult, ALU.add)),
                            reads=xbread + K("convw") + K("convb"),
                            writes=xbc_k(cg))
                    else:
                        add("v", (lambda e, sc=xbc_v(cg), s=xb_v(cg, k, CH + k),
                                  wk=sv["convw"][0][:, k * NCG + cg:k * NCG + cg + 1]:
                                  e.scalar_tensor_tensor(sc, s, wk, sc,
                                                         ALU.mult, ALU.add)),
                            reads=xbread + K("convw") + xbc_k(cg),
                            writes=xbc_k(cg))

            for k in range(4):
                conv_sweep(k)
                wy_group()
                wy_group()

            add("q", (lambda e, d=x_dram:
                      e.dma_start(
                          out=rg1_t.ap().rearrange("p (g n) -> p g n",
                                                   g=NCG, n=CHH)[:, :, 0:CH],
                          in_=d.rearrange("(g p) n -> p g n",
                                          p=128, g=NCG)[:, :, 3:CHH])),
                writes=R1.kall(), kind="dma", dma_sem="x")

            set_pool(SCR_MAIN)

            def gate_group(wname, mc, out_emit):
                slot = slab_small(wname, mc)
                p2 = [ps_next(), ps_next()]
                mm_group(K("wsm", slot), wsm[slot], NCG,
                         lambda k, a, b: xbc_v(k, a, b),
                         lambda k, a, b: xbc_k(k, a, b), G_MAIN, p2)
                sv_, sk_ = scr_next()
                sb16 = sv_(0, 512).bitcast(BF16)
                for gi, (c0, c1) in enumerate(G_MAIN):
                    add("s", (lambda e, p=ps[p2[gi]], sc=sb16[:, c0:c1], w=c1 - c0:
                              e.activation(sc, p[:, 0:w], AF.Sigmoid)),
                        reads=KP(p2[gi]), writes=sk_(0, 512))
                out_emit(sb16, lambda: sk_(0, 512))

            def wa_out(mc):
                def go(sb16, skf):
                    add("v", (lambda e, sc=sb16, d=la_v(mc),
                              g=sv_gneg[:, mc:mc + 1]:
                              e.tensor_scalar(d, sc, g, None, ALU.mult)),
                        reads=skf() + K("gneg"), writes=la_k(mc))
                return go

            def wi_out(mc):
                def go(sb16, skf):
                    add("v", (lambda e, sc=sb16, d=ixb_v(mc), x=xbc_v(mc):
                              e.tensor_tensor(d, sc, x, ALU.mult)),
                        reads=skf() + xbc_k(mc), writes=ixb_k(mc))
                return go

            def gelu_unit(g_v, g_k):
                tAf, tAk = scr_next()
                tBf, tBk = scr_next()
                tA = tAf(0, 512).bitcast(BF16)
                tB = tBf(0, 512).bitcast(BF16)
                kA = lambda a=0, b=CH: tAk(0, 512)
                kB = lambda a=0, b=CH: tBk(0, 512)
                add("s", (lambda e, sc=tA, g=g_v: e.activation(sc, g, AF.Square)),
                    reads=g_k, writes=kA())
                add("s", (lambda e, sc=tA, sd=tB:
                          e.activation(sd, sc, AF.Identity,
                                       scale=2.0 * GC * GA, bias=2.0 * GC)),
                    reads=kA() + K("consts"), writes=kB())
                add("v", (lambda e, sc=tA, sd=tB, g=g_v:
                          e.tensor_tensor(sc, g, sd, ALU.mult)),
                    reads=g_k + kB(), writes=kA())
                add("s", (lambda e, sc=tA, sd=tB:
                          e.activation(sd, sc, AF.Sigmoid)),
                    reads=kA(), writes=kB())
                return (lambda a=0, b=CH: tA[:, a:b]), kA, (lambda a=0, b=CH: tB[:, a:b]), kB

            for mc in range(NCG):
                gate_group("wa", mc, wa_out(mc))

            def sig_half(mcs):
                for step in range(len(mcs) // 2):
                    gate_group("wi", mcs[2 * step], wi_out(mcs[2 * step]))
                    gate_group("wi", mcs[2 * step + 1], wi_out(mcs[2 * step + 1]))
                    for mc in (mcs[2 * step], mcs[2 * step + 1]):
                        ykall = yb2_k(mc) + tuple(
                            k + f":h{gi}" for gi in range(2)
                            for k in yb2_k(mc))
                        tA, tAk, tB, tBk = gelu_unit(yb2_v(mc), ykall)
                        add("v", (lambda e, g=yb2_v(mc), sd=tB():
                                  e.tensor_tensor(g, g, sd, ALU.mult)),
                            reads=ykall + tBk(), writes=ykall)

            sig_half([0, 1])
            set_pool(SCR_MAIN + SCR_R6)
            add("v", (lambda e: e.memset(rg6_t[:, 0:CH], 0.0)),
                reads=(), writes=R6.k(0, CH))

            def d3_pair(mcs):
                u = {}
                for mc in mcs:
                    tiles = []
                    for _ in range(3):
                        tv, tk = scr_next()
                        tiles.append(((lambda a=0, b=CH, tv=tv:
                                       tv(0, CH).bitcast(BF16)[:, a:b]), tk))
                    u[mc] = tuple(tiles)
                for mc in mcs:
                    add("v", (lambda e, l=la_v(mc), mc=mc:
                              e.reduce_sum(ends[:, mc:mc + 1], l,
                                           axis=mybir.AxisListType.X)),
                        reads=la_k(mc), writes=(f"ends:c{mc}",))
                for mc in mcs:
                    (sa, sak), _, _ = u[mc]
                    add("s", (lambda e, sc=sa(), l=la_v(mc):
                              e.activation(sc, l, AF.Exp)),
                        reads=la_k(mc), writes=sak())
                for mc in mcs:
                    (sa, sak), (sb2, sb2k), _ = u[mc]
                    add("v", (lambda e, sa_=sa(), sc=sb2():
                              e.tensor_tensor(sc, sa_, sa_, ALU.mult)),
                        reads=sak() + sb2k(), writes=sb2k())
                for mc in mcs:
                    _, (sb2, sb2k), _ = u[mc]
                    add("v", (lambda e, sc=sb2():
                              e.tensor_scalar(sc, sc, -1.0, 1.0, ALU.mult, ALU.add)),
                        reads=sb2k(), writes=sb2k())
                for mc in mcs:
                    _, (sb2, sb2k), _ = u[mc]
                    add("s", (lambda e, sc=sb2(): e.activation(sc, sc, AF.Ln)),
                        reads=sb2k(), writes=sb2k())
                for mc in mcs:
                    _, (sb2, sb2k), _ = u[mc]
                    add("s", (lambda e, sc=sb2():
                              e.activation(sc, sc, AF.Exp, scale=0.5)),
                        reads=sb2k() + K("consts"), writes=sb2k())
                for mc in mcs:
                    _, (sb2, sb2k), (sbb, sbbk) = u[mc]
                    add("v", (lambda e, sc=sb2(), sd=sbb(), i=ixb_v(mc):
                              e.scalar_tensor_tensor(sd, sc, 1e-12, i,
                                                     ALU.max, ALU.mult)),
                        reads=sb2k() + ixb_k(mc), writes=sbbk())
                for mc in mcs:
                    (sa, sak), _, (sbb, sbbk) = u[mc]
                    add("v", (lambda e, sc=sa(), sd=sbb(), d=h_v(mc):
                              e.tensor_tensor_scan(d, sc, sd, 0.0,
                                                   ALU.mult, ALU.add)),
                        reads=sak() + sbbk(), writes=h_k(mc))

            d3_pair([0, 1])
            sig_half([2, 3])
            d3_pair([2, 3])
            sig_half([4, 5])
            d3_pair([4, 5])
            sig_half([6, 7])
            set_pool(SCR_MAIN + SCR_R6 + SCR_XC)
            d3_pair([6, 7])

            add("s", (lambda e: e.activation(ends[:, 0:8], ends[:, 0:8], AF.Exp)),
                reads=tuple(f"ends:c{i}" for i in range(8)),
                writes=tuple(f"ends:c{i}" for i in range(8)))
            for mc in range(NCG):
                add("v", (lambda e, mc=mc, h=h_v(mc):
                          e.tensor_copy(ends[:, 8 + mc:9 + mc], h[:, CH - 1:CH])),
                    reads=h_k(mc), writes=(f"ends:c{8 + mc}",))
            pT = ps_next()
            add("pe", (lambda e, p=ps[pT]:
                       e.transpose(p[0:16, 0:128], ends[:, :], sv["ident"][0][:, :])),
                reads=tuple(f"ends:c{i}" for i in range(16)) + K("ident"),
                writes=KP(pT))
            add("s", (lambda e, p=ps[pT]: e.copy(endsT[:, :], p[0:16, 0:128])),
                reads=KP(pT), writes=K_endsT)
            add("q", (lambda e, d=ag_in[pi]: e.dma_start(out=d.ap(), in_=endsT[:, :])),
                reads=K_endsT, writes=K(f"agin{pi}"), kind="dma",
                dma_sem=f"ag{pi}")
            cop = add("p", (lambda e, i=pi:
                            e.collective_compute(
                                "AllGather", ALU.bypass,
                                replica_groups=[list(range(N_CORES))],
                                ins=[ag_in[i].ap().opt()],
                                outs=[ag_out[i].ap().opt()])),
                      reads=K(f"agin{pi}"), writes=K(f"agout{pi}"), kind="coll")
            cop.dma_sem = cc[pi]
            cop.dma_val = 1
            for mc in range(NCG):
                sa2, sa2k = scr_next()
                add("s", (lambda e, sc=sa2(), l=la_v(mc):
                          e.activation(sc, l, AF.Exp)),
                    reads=la_k(mc), writes=sa2k())
                add("v", (lambda e, sc=sa2(), d=A_v(mc):
                          e.tensor_tensor_scan(d, sc, rg6_t[:, 0:CH], 1.0,
                                               ALU.mult, ALU.add)),
                    reads=sa2k() + R6.k(0, CH), writes=A_k(mc))
            add("q", (lambda e, d=ag_out[pi]: e.dma_start(out=gT[:, :], in_=d.ap())),
                reads=K(f"agout{pi}"), writes=K_gTall, kind="dma",
                dma_sem=f"ag{pi}")
            pT2 = ps_next()
            add("pe", (lambda e, p=ps[pT2]:
                       e.transpose(p[:, 0:128], gT[:, :], sv["ident"][0][:, :])),
                reads=K_gTall + K("ident"), writes=KP(pT2))
            add("s", (lambda e, p=ps[pT2]: e.copy(gAll[:, :], p[:, 0:128])),
                reads=KP(pT2), writes=K_gTall)

            def fold(mname, dstt, dkey):
                add("v", (lambda e, m=sv[mname][0]: e.tensor_tensor(
                    fsc[0][:, :], gAll[:, :], m[:, :], ALU.mult)),
                    reads=K_gTall + K(mname), writes=K_fsc[0])
                add("v", (lambda e: e.tensor_tensor(
                    fsc[1][:, 0:64], fsc[0][:, 0:64], fsc[0][:, 64:128], ALU.add)),
                    reads=K_fsc[0], writes=K_fsc[1])
                add("v", (lambda e: e.tensor_tensor(
                    fsc[0][:, 0:32], fsc[1][:, 0:32], fsc[1][:, 32:64], ALU.add)),
                    reads=K_fsc[1], writes=K_fsc[0])
                add("v", (lambda e, d=dstt: e.tensor_tensor(
                    d[:, 0:16], fsc[0][:, 0:16], fsc[0][:, 16:32], ALU.add)),
                    reads=K_fsc[0], writes=dkey)

            if pi == 0:
                fold("mask_e", foldE, K("foldE"))
                fold("mask_o", foldO, K("foldO"))
                add("v", (lambda e: e.tensor_tensor(
                    tiny[0][:, :], foldO[:, 0:8], foldE[:, 8:16], ALU.mult)),
                    reads=K("foldO") + K("foldE"), writes=K("tiny", 0))
                add("v", (lambda e: e.tensor_tensor(
                    H2[:, :], tiny[0][:, :], foldO[:, 8:16], ALU.add)),
                    reads=K("tiny", 0) + K("foldO"), writes=K("H2"))
                add("v", (lambda e: e.tensor_scalar(
                    hin[:, :], foldE[:, 8:16], sv["slf"][0][:, 0:1], None, ALU.mult)),
                    reads=K("foldE") + K("slf"), writes=K("hin"))
            else:
                fold("mask_e", foldE, K("foldE"))
                add("v", (lambda e: e.tensor_tensor(
                    tiny[0][:, :], foldE[:, 0:8], H2[:, :], ALU.mult)),
                    reads=K("foldE") + K("H2"), writes=K("tiny", 0))
                add("v", (lambda e: e.tensor_tensor(
                    tiny[0][:, :], tiny[0][:, :], foldE[:, 8:16], ALU.add)),
                    reads=K("tiny", 0) + K("foldE"), writes=K("tiny", 0))
                add("v", (lambda e: e.tensor_scalar(
                    tiny[1][:, :], H2[:, :], sv["slf"][0][:, 1:2], None, ALU.mult)),
                    reads=K("H2") + K("slf"), writes=K("tiny", 1))
                add("v", (lambda e: e.tensor_scalar(
                    tiny[0][:, :], tiny[0][:, :], sv["slf"][0][:, 0:1], None, ALU.mult)),
                    reads=K("tiny", 0) + K("slf"), writes=K("tiny", 0))
                add("v", (lambda e: e.tensor_tensor(
                    hin[:, :], tiny[0][:, :], tiny[1][:, :], ALU.add)),
                    reads=K("tiny", 0) + K("tiny", 1), writes=K("hin"))

            for mc in range(NCG):
                add("v", (lambda e, a=A_v(mc), d=h_v(mc), s=hin[:, mc:mc + 1]:
                          e.scalar_tensor_tensor(d, a, s, d, ALU.mult, ALU.add)),
                    reads=A_k(mc) + K("hin") + h_k(mc), writes=h_k(mc))
            for mc in range(NCG):
                add("v", (lambda e, d=hyb_v(mc), h=h_v(mc), y=yb2_v(mc):
                          e.tensor_tensor(d, h, y, ALU.mult)),
                    reads=h_k(mc) + yb2_k(mc), writes=hyb_k(mc))

            set_pool(SCR_MAIN)
            rms2_st = norm_stats_begin(False)
            for mc in range(NCG):
                slot = slab_small("wo", mc)
                p2 = [ps_next(), ps_next()]
                mm_group(K("wsm", slot), wsm[slot], NCG,
                         lambda k, a, b: hyb_v(k, a, b),
                         lambda k, a, b: hyb_k(k, a, b), G_MAIN, p2)
                for gi, (c0, c1) in enumerate(G_MAIN):
                    add("v", (lambda e, p=ps[p2[gi]], x=x_v(mc, c0, c1),
                              d=h_v(mc, c0, c1), w=c1 - c0:
                              e.tensor_tensor(d, p[:, 0:w], x, ALU.add)),
                        reads=KP(p2[gi]) + x_k(mc, c0, c1),
                        writes=h_k(mc, c0, c1))
                add("v", (lambda e, d=ln1_v(mc), h=h_v(mc), x=x_v(mc, 0, CH):
                          e.tensor_tensor(d, h, x, ALU.add)),
                    reads=h_k(mc) + x_k(mc, 0, CH), writes=ln1_k(mc))
            for mc in range(NCG):
                norm_stats_cg(rms2_st, mc,
                              lambda cg, a, b: h_v(cg, a, b),
                              lambda cg, a, b: h_k(cg, a, b),
                              mc == 0, mc == NCG - 1)

            rms_finish(rms2_st,
                       lambda cg, a, b: h_v(cg, a, b),
                       lambda cg, a, b: h_k(cg, a, b),
                       sv["norm2w"][0], K("norm2w"),
                       lambda cg, a, b: u_v(cg, a, b),
                       lambda cg, a, b: u_k(cg, a, b), EPS_RMS)

            set_pool(SCR_MAIN + SCR_LA)
            for half in range(2):
                for j in range(16):
                    m2 = half * 16 + j
                    slot_g = slab_small("wg", m2)
                    pg = [ps_next(), ps_next()]
                    mm_group(K("wsm", slot_g), wsm[slot_g], NCG,
                             lambda k, a, b: u_v(k, a, b),
                             lambda k, a, b: u_k(k, a, b), G_MAIN, pg)
                    slot_u = slab_small("wu", m2)
                    pu = [ps_next(), ps_next()]
                    mm_group(K("wsm", slot_u), wsm[slot_u], NCG,
                             lambda k, a, b: u_v(k, a, b),
                             lambda k, a, b: u_k(k, a, b), G_MAIN, pu)
                    for gi, (c0, c1) in enumerate(G_MAIN):
                        add("s", (lambda e, p=ps[pg[gi]],
                                  d=prod_v(j, c0, c1), w=c1 - c0:
                                  e.copy(d, p[:, 0:w])),
                            reads=KP(pg[gi]), writes=prod_k(j, c0, c1))
                    tA, tAk, tB, tBk = gelu_unit(prod_v(j), prod_k(j))
                    add("v", (lambda e, sc=tA(), sd=tB(), g=prod_v(j):
                              e.tensor_tensor(sc, g, sd, ALU.mult)),
                        reads=prod_k(j) + tBk(), writes=tAk())
                    for gi, (c0, c1) in enumerate(G_MAIN):
                        add("v", (lambda e, p=ps[pu[gi]], sc=tA(c0, c1),
                                  d=prod_v(j, c0, c1), w=c1 - c0:
                                  e.tensor_tensor(d, sc, p[:, 0:w], ALU.mult)),
                            reads=KP(pu[gi]) + tAk(c0, c1),
                            writes=prod_k(j, c0, c1))

                if half == 1:
                    set_pool(SCR_MAIN)
                    ln1_st = norm_stats_begin(True)
                for md in range(NCG):
                    slot = slab_big("wd", md, half)
                    p2 = [ps_next(), ps_next()]
                    mm_group(K("wbg", slot), wbg[slot], 16,
                             lambda k, a, b: prod_v(k, a, b),
                             lambda k, a, b: prod_k(k, a, b), G_MAIN, p2)
                    for gi, (c0, c1) in enumerate(G_MAIN):
                        if half == 0:
                            add("v", (lambda e, p=ps[p2[gi]],
                                      d=x_v(md, c0, c1), w=c1 - c0:
                                      e.tensor_copy(d, p[:, 0:w])),
                                reads=KP(p2[gi]), writes=x_k(md, c0, c1))
                        else:
                            add("v", (lambda e, p=ps[p2[gi]],
                                      d=x_v(md, c0, c1), w=c1 - c0:
                                      e.tensor_tensor(d, p[:, 0:w], d, ALU.add)),
                                reads=KP(p2[gi]) + x_k(md, c0, c1),
                                writes=x_k(md, c0, c1))
                    if half == 1:
                        add("v", (lambda e, d=ln1_v(md), wv=x_v(md, 0, CH):
                                  e.tensor_tensor(d, wv, d, ALU.add)),
                            reads=x_k(md, 0, CH) + ln1_k(md), writes=ln1_k(md))
            for md in range(NCG):
                norm_stats_cg(ln1_st, md,
                              lambda cg, a, b: ln1_v(cg, a, b),
                              lambda cg, a, b: ln1_k(cg, a, b),
                              md == 0, md == NCG - 1)

            ln_finish(ln1_st,
                      lambda cg, a, b: ln1_v(cg, a, b),
                      lambda cg, a, b: ln1_k(cg, a, b),
                      lambda cg, a, b: x_v(cg, a, b),
                      lambda cg, a, b: x_k(cg, a, b), EPS_LN,
                      extra=(lambda cg, a, b: u_v(cg, a, b),
                             lambda cg, a, b: u_k(cg, a, b)))

            for half in range(2):
                for j in range(16):
                    m2 = half * 16 + j
                    slot = slab_small("w1", m2)
                    p2 = [ps_next(), ps_next()]
                    mm_group(K("wsm", slot), wsm[slot], NCG,
                             lambda k, a, b: u_v(k, a, b),
                             lambda k, a, b: u_k(k, a, b), G_MAIN, p2)
                    for gi, (c0, c1) in enumerate(G_MAIN):
                        add("s", (lambda e, p=ps[p2[gi]], d=prod_v(j, c0, c1),
                                  bv=sv["b1v"][0][:, m2:m2 + 1], w=c1 - c0:
                                  e.activation(d, p[:, 0:w], AF.Relu, bias=bv)),
                            reads=KP(p2[gi]) + K("b1v"),
                            writes=prod_k(j, c0, c1))
                if half == 1:
                    set_pool(SCR_MAIN)
                    ln2_st = norm_stats_begin(True)
                for md in range(NCG):
                    slot = slab_big("w2", md, half)
                    p2 = [ps_next(), ps_next()]
                    mm_group(K("wbg", slot), wbg[slot], 16,
                             lambda k, a, b: prod_v(k, a, b),
                             lambda k, a, b: prod_k(k, a, b), G_MAIN, p2)
                    for gi, (c0, c1) in enumerate(G_MAIN):
                        if half == 0:
                            add("v", (lambda e, p=ps[p2[gi]],
                                      d=h_v(md, c0, c1), w=c1 - c0:
                                      e.tensor_copy(d, p[:, 0:w])),
                                reads=KP(p2[gi]), writes=h_k(md, c0, c1))
                        else:
                            add("v", (lambda e, p=ps[p2[gi]],
                                      d=h_v(md, c0, c1), w=c1 - c0:
                                      e.tensor_tensor(d, p[:, 0:w], d, ALU.add)),
                                reads=KP(p2[gi]) + h_k(md, c0, c1),
                                writes=h_k(md, c0, c1))
                    if half == 1:
                        add("v", (lambda e, d=h_v(md), bv=sv["b2v"][0][:, md:md + 1],
                                  x2=x_v(md, 0, CH):
                                  e.scalar_tensor_tensor(d, d, bv, x2,
                                                         ALU.add, ALU.add)),
                            reads=h_k(md) + K("b2v") + x_k(md, 0, CH),
                            writes=h_k(md))
            if head_next is not None:
                head_next()
                set_pool(SCR_MAIN)
            for md in range(NCG):
                norm_stats_cg(ln2_st, md,
                              lambda cg, a, b: h_v(cg, a, b),
                              lambda cg, a, b: h_k(cg, a, b),
                              md == 0, md == NCG - 1)

            ln_finish(ln2_st,
                      lambda cg, a, b: h_v(cg, a, b),
                      lambda cg, a, b: h_k(cg, a, b),
                      lambda cg, a, b: SCR_XC[cg % 4][0](a, b),
                      lambda cg, a, b: SCR_XC[cg % 4][1](a, b), EPS_LN,
                      stream_out=ys[pi])

        pass_head(0)
        do_pass(0, head_next=lambda: pass_head(1))
        do_pass(1)

        P.number()
        block = E(nc.Block())

        @block.tensor
        def _(t):
            P.emit_engine("pe", t)

        @block.vector
        def _(v):
            P.emit_engine("v", v)

        @block.scalar
        def _(s):
            P.emit_engine("s", s)

        @block.gpsimd
        def _(g):
            P.emit_engine("p", g)

        @block.sync
        def _(q):
            P.emit_engine("q", q)
            q.wait_ge(dsem["y0"], P.dma_counts.get("y0", 0))
            q.wait_ge(dsem["y1"], P.dma_counts.get("y1", 0))

    return nc


def _tile_w(w):
    Kd, Md = w.shape
    t = w.reshape(Kd // 128, 128, Md // 128, 128).transpose(2, 1, 0, 3)
    return np.ascontiguousarray(t.reshape(Md // 128 * 128, Kd))


def _to_fm(v):
    return np.ascontiguousarray(v.reshape(-1, 128).T)


_NC_CACHE = {}


def make_in_maps(L0_attn_x, L0_attn_norm1_w, L0_attn_Wx, L0_attn_Wy,
                 L0_attn_conv_w, L0_attn_conv_b, L0_attn_Wi, L0_attn_Wa,
                 L0_attn_lam, L0_attn_Wo, L0_attn_norm2_w, L0_attn_Wg,
                 L0_attn_Wu, L0_attn_Wd, L0_ffn_W1, L0_ffn_b1, L0_ffn_W2,
                 L0_ffn_b2):
    x = np.asarray(L0_attn_x, dtype=np.float32)

    shared = dict(
        wx=_tile_w(np.asarray(L0_attn_Wx, np.float32)),
        wy=_tile_w(np.asarray(L0_attn_Wy, np.float32)),
        wa=_tile_w(np.asarray(L0_attn_Wa, np.float32)),
        wi=_tile_w(np.asarray(L0_attn_Wi, np.float32)),
        wo=_tile_w(np.asarray(L0_attn_Wo, np.float32)),
        wg=_tile_w(np.asarray(L0_attn_Wg, np.float32)),
        wu=_tile_w(np.asarray(L0_attn_Wu, np.float32)),
        wd=_tile_w(np.asarray(L0_attn_Wd, np.float32)),
        w1=_tile_w(np.asarray(L0_ffn_W1, np.float32)),
        w2=_tile_w(np.asarray(L0_ffn_W2, np.float32)),
        norm1w=_to_fm(np.asarray(L0_attn_norm1_w, np.float32)),
        norm2w=_to_fm(np.asarray(L0_attn_norm2_w, np.float32)),
        convb=_to_fm(np.asarray(L0_attn_conv_b, np.float32)),
        convw=np.ascontiguousarray(np.concatenate(
            [_to_fm(np.asarray(L0_attn_conv_w, np.float32)[k])
             for k in range(4)], axis=1)),
        lam=_to_fm(np.asarray(L0_attn_lam, np.float32)),
        b1v=_to_fm(np.asarray(L0_ffn_b1, np.float32)),
        b2v=_to_fm(np.asarray(L0_ffn_b2, np.float32)),
        ident=np.eye(128, dtype=np.float32),
    )

    in_maps = []
    for c in range(N_CORES):
        pair, role = c // 2, c % 2
        m = dict(shared)
        for pi in range(2):
            ck = 2 * pi + role
            t0 = ck * CH
            if t0 == 0:
                sl = np.concatenate(
                    [np.zeros((3, D), np.float32), x[pair, 0:CH, :]], axis=0)
            else:
                sl = x[pair, t0 - 3:t0 + CH, :]
            m[f"x{pi}"] = np.ascontiguousarray(sl.T)
        me = np.zeros((128, 128), np.float32)
        mo = np.zeros((128, 128), np.float32)
        me[:, (2 * pair) * 16:(2 * pair + 1) * 16] = 1.0
        mo[:, (2 * pair + 1) * 16:(2 * pair + 2) * 16] = 1.0
        m["mask_e"] = me
        m["mask_o"] = mo
        sl2 = np.zeros((128, 2), np.float32)
        sl2[:, 0] = float(role)
        sl2[:, 1] = 1.0 - float(role)
        m["slf"] = sl2
        in_maps.append(m)
    return in_maps


def assemble(results):
    out = np.empty((B, SEQ, D), dtype=np.float32)
    for c in range(N_CORES):
        pair, role = c // 2, c % 2
        for pi in range(2):
            ck = 2 * pi + role
            y = results[c][f"y{pi}"]
            out[pair, ck * CH:(ck + 1) * CH, :] = y.T
    return out


def kernel(**inputs):
    in_maps = make_in_maps(**inputs)
    if "nc" not in _NC_CACHE:
        _NC_CACHE["nc"] = build_nc()
    nc = _NC_CACHE["nc"]
    res = run_bass_kernel_spmd(nc, in_maps, core_ids=list(range(N_CORES)))
    return assemble(res.results)


def _dead(results):
    out = np.empty((B, SEQ, D), dtype=np.float32)
    return out


# version 28
# speedup vs baseline: 1.0417x; 1.0078x over previous
import contextlib
import numpy as np

import concourse.bass as bass
import concourse.mybir as mybir
from concourse.bass_utils import run_bass_kernel_spmd

F32 = mybir.dt.float32
BF16 = mybir.dt.bfloat16
AF = mybir.ActivationFunctionType
ALU = mybir.AluOpType

D = 1024
HID = 4096
B = 4
SEQ = 4096
CH = 1024
CHH = CH + 3
NCG = D // 128
NH = HID // 128
N_CORES = 8
EPS_RMS = 1e-6
EPS_LN = 1e-5
GC = 0.7978845608028654
GA = 0.044715

G_HALO = [(0, 3), (3, 515), (515, 1027)]
G_MAIN = [(0, 512), (512, 1024)]


class Op:
    __slots__ = ("engine", "emit", "reads", "writes", "kind", "needs_inc",
                 "inc_val", "waits", "dma_sem", "dma_val", "dma_stream", "seq")

    def __init__(self, engine, emit, reads, writes, kind):
        self.engine = engine
        self.emit = emit
        self.reads = reads
        self.writes = writes
        self.kind = kind
        self.needs_inc = False
        self.inc_val = None
        self.waits = []
        self.dma_sem = None
        self.dma_val = None


class Plan:
    def __init__(self):
        self.ops = {e: [] for e in ("pe", "v", "s", "p", "q")}
        self.last_writer = {}
        self.readers = {}
        self.since_drain = {"v": set(), "s": set()}
        self.eng_sems = {}
        self.dma_sems = {}
        self.dma_counts = {}

    def _dep(self, op, dep):
        if dep is op:
            return
        if dep.kind == "coll":
            op.waits.append(("sem", dep.dma_sem, dep.dma_val))
            return
        if dep.kind == "dma":
            op.waits.append(("sem", self.dma_sems[dep.dma_stream],
                             self.dma_counts[dep.dma_stream]))
            return
        if dep.engine == op.engine:
            if op.engine == "pe":
                return
            if op.kind in ("dma", "coll") or op.engine == "p":
                op.waits.append(dep)
                return
            if id(dep) in self.since_drain.get(op.engine, set()):
                d = Op(op.engine, None, (), (), "drain")
                self.ops[op.engine].append(d)
                self.since_drain[op.engine] = set()
            return
        op.waits.append(dep)

    def add(self, engine, emit, reads=(), writes=(), kind="compute",
            dma_sem=None):
        op = Op(engine, emit, tuple(reads), tuple(writes), kind)
        for k in op.reads:
            lw = self.last_writer.get(k)
            if lw is not None:
                self._dep(op, lw)
        for k in op.writes:
            lw = self.last_writer.get(k)
            if lw is not None:
                self._dep(op, lw)
            for r in self.readers.get(k, ()):
                self._dep(op, r)
        if kind == "dma":
            op.dma_stream = dma_sem
            self.dma_counts[dma_sem] = self.dma_counts.get(dma_sem, 0) + 16
            op.dma_sem = self.dma_sems[dma_sem]
            op.dma_val = self.dma_counts[dma_sem]
        op.seq = len(self.ops[engine])
        self.ops[engine].append(op)
        for k in op.writes:
            self.last_writer[k] = op
            self.readers[k] = []
        if engine in ("v", "s") and kind == "compute":
            self.since_drain[engine].add(id(op))
        for k in op.reads:
            self.readers.setdefault(k, []).append(op)
        return op

    def number(self):
        for e in self.ops:
            for op in self.ops[e]:
                best = {}
                rest = []
                for dep in op.waits:
                    if isinstance(dep, tuple):
                        rest.append(dep)
                        continue
                    b = best.get(dep.engine)
                    if b is None or dep.seq > b.seq:
                        best[dep.engine] = dep
                for dep in best.values():
                    dep.needs_inc = True
                op.waits = rest + list(best.values())
        for e in self.ops:
            cnt = 0
            for op in self.ops[e]:
                if op.needs_inc and op.kind == "compute":
                    cnt += 1
                    op.inc_val = cnt

    def emit_engine(self, ename, eng):
        hwm = {}
        for op in self.ops[ename]:
            if op.kind == "drain":
                eng.drain()
                continue
            for dep in op.waits:
                if isinstance(dep, tuple):
                    _, sem, val = dep
                else:
                    sem, val = self.eng_sems[dep.engine], dep.inc_val
                    assert val is not None
                if hwm.get(id(sem), -1) < val:
                    eng.wait_ge(sem, val)
                    hwm[id(sem)] = val
            inst = op.emit(eng)
            if op.kind == "dma":
                inst.then_inc(op.dma_sem, 16)
            elif op.kind == "coll":
                inst.then_inc(op.dma_sem, op.dma_val)
            elif op.needs_inc:
                inst.then_inc(self.eng_sems[op.engine], 1)


class Region:
    PAGE = 512

    def __init__(self, tensor, name, n_elems):
        self.t = tensor
        self.name = name
        self.n = n_elems

    def v(self, c0, c1):
        return self.t[:, c0:c1]

    def k(self, c0, c1):
        return tuple(f"{self.name}:{i}"
                     for i in range(c0 // self.PAGE, (c1 + self.PAGE - 1) // self.PAGE))

    def kall(self):
        return self.k(0, self.n)


def build_nc():
    nc = bass.Bass("TRN2", target_bir_lowering=False)

    def din(name, shape, dt=F32):
        return nc.dram_tensor(name, shape, dt, kind="ExternalInput").ap()

    xs = [din("x0", [D, CHH]), din("x1", [D, CHH])]
    wdr = {n: din(n, [NCG * 128, D]) for n in ("wx", "wy", "wa", "wi", "wo")}
    wdr.update({n: din(n, [NH * 128, D]) for n in ("wg", "wu", "w1")})
    wdr.update({n: din(n, [NCG * 128, HID]) for n in ("wd", "w2")})
    norm1w = din("norm1w", [128, NCG])
    norm2w = din("norm2w", [128, NCG])
    convb = din("convb", [128, NCG])
    convw = din("convw", [128, 4 * NCG])
    lam = din("lam", [128, NCG])
    b1v = din("b1v", [128, NH])
    b2v = din("b2v", [128, NCG])
    mask_e = din("mask_e", [128, 128])
    mask_o = din("mask_o", [128, 128])
    slf = din("slf", [128, 2])
    ident = din("ident", [128, 128])

    ys = [nc.dram_tensor(f"y{i}", [D, CH], F32, kind="ExternalOutput").ap()
          for i in range(2)]
    ag_in = [nc.dram_tensor(f"ag{i}_in", [16, 128], F32) for i in range(2)]
    ag_out = [nc.dram_tensor(f"ag{i}_out", [128, 128], F32) for i in range(2)]

    with contextlib.ExitStack() as ctx:
        E = ctx.enter_context

        rg1_t = E(nc.sbuf_tensor("rg1", [128, NCG * CHH], F32))
        rg2_t = E(nc.sbuf_tensor("rg2", [128, NCG * CH], F32))
        rg3_t = E(nc.sbuf_tensor("rg3", [128, NCG * CH], F32))
        rg5_t = E(nc.sbuf_tensor("rg5", [128, NCG * CHH + 3 * NCG * CH], BF16))
        rg6_t = E(nc.sbuf_tensor("rg6", [128, NCG * CHH], BF16))
        R1 = Region(rg1_t, "rg1", NCG * CHH)
        R2 = Region(rg2_t, "rg2", NCG * CH)
        R3 = Region(rg3_t, "rg3", NCG * CH)
        R5 = Region(rg5_t, "rg5", NCG * CHH + 3 * NCG * CH)
        R5f = Region(rg5_t.ap().bitcast(F32), "rg5", (NCG * CHH + 3 * NCG * CH) // 2)
        R5f.PAGE = 256
        R3b = Region(rg3_t.ap().bitcast(BF16), "rg3", 2 * NCG * CH)
        R3b.PAGE = 1024
        R6 = Region(rg6_t, "rg6", NCG * CHH)

        scr_t = [E(nc.sbuf_tensor(f"scr{i}", [128, CH], F32)) for i in range(3)]
        wsm = [E(nc.sbuf_tensor(f"wsm{i}", [128, D], BF16)) for i in range(3)]
        wbg = [E(nc.sbuf_tensor(f"wbg{i}", [128, 2048], BF16)) for i in range(2)]

        sv = {}
        for name, dram, wdt in [
            ("norm1w", norm1w, NCG), ("norm2w", norm2w, NCG),
            ("convb", convb, NCG), ("convw", convw, 4 * NCG),
            ("lam", lam, NCG), ("b1v", b1v, NH), ("b2v", b2v, NCG),
            ("mask_e", mask_e, 128), ("mask_o", mask_o, 128),
            ("slf", slf, 2), ("ident", ident, 128),
        ]:
            sv[name] = (E(nc.sbuf_tensor("sv_" + name, [128, wdt], F32)), dram)
        sv_gneg = E(nc.sbuf_tensor("sv_gneg", [128, NCG], F32))
        CONST_VALS = [1.0, -1.0, 0.5, -0.5, 2.0, EPS_RMS, EPS_LN,
                      2.0 * GC, 2.0 * GC * GA]
        constv = E(nc.sbuf_tensor("constv", [128, len(CONST_VALS)], F32))
        for _i, _val in enumerate(CONST_VALS):
            nc.const_aps.register(F32, _val, constv[:, _i:_i + 1])
        inv1024 = E(nc.sbuf_tensor("inv1024", [128, 1], F32))
        inv1024b = E(nc.sbuf_tensor("inv1024b", [128, 1], BF16))
        ones_row = E(nc.sbuf_tensor("ones_row", [1, 128], F32))
        ones_rowb = E(nc.sbuf_tensor("ones_rowb", [1, 128], BF16))
        ends = E(nc.sbuf_tensor("ends", [128, 16], F32))
        foldE = E(nc.sbuf_tensor("foldE", [128, 16], F32))
        foldO = E(nc.sbuf_tensor("foldO", [128, 16], F32))
        H2 = E(nc.sbuf_tensor("H2", [128, NCG], F32))
        hin = E(nc.sbuf_tensor("hin", [128, NCG], F32))
        tiny = [E(nc.sbuf_tensor(f"tiny{i}", [128, NCG], F32)) for i in range(2)]

        ps = [E(nc.psum_tensor(f"ps{i}", [128, 512], F32)) for i in range(8)]

        sem_pe = E(nc.semaphore("sem_pe"))
        sem_v = E(nc.semaphore("sem_v"))
        sem_s = E(nc.semaphore("sem_s"))
        sem_p = E(nc.semaphore("sem_p"))
        DSTREAMS = (["init", "x", "ag0", "ag1", "y0", "y1"]
                    + [f"wsm{i}" for i in range(3)]
                    + [f"wbg{i}" for i in range(2)])
        dsem = {n: E(nc.semaphore("d_" + n)) for n in DSTREAMS}
        cc = [E(nc.semaphore(f"cc{i}")) for i in range(2)]

        P = Plan()
        P.eng_sems = {"pe": sem_pe, "v": sem_v, "s": sem_s, "p": sem_p}
        P.dma_sems = dsem
        add = P.add

        def x_v(cg, a, b):
            return R1.v(cg * CHH + a, cg * CHH + b)

        def x_k(cg, a, b):
            return R1.k(cg * CHH + a, cg * CHH + b)

        def h_v(cg, a=0, b=CH):
            return R2.v(cg * CH + a, cg * CH + b)

        def h_k(cg, a=0, b=CH):
            return R2.k(cg * CH + a, cg * CH + b)

        def ln1_v(cg, a=0, b=CH):
            return R3.v(cg * CH + a, cg * CH + b)

        def ln1_k(cg, a=0, b=CH):
            return R3.k(cg * CH + a, cg * CH + b)

        def A_v(cg):
            return R3b.v(cg * CH, (cg + 1) * CH)

        def A_k(cg):
            return R3b.k(cg * CH, (cg + 1) * CH)

        OXB = 0
        OXBC = NCG * CHH
        OLA = OXBC + NCG * CH
        OIXB = OLA + NCG * CH
        ROWB = OLA // 2
        ROWB2 = ROWB + 1028
        ROWT = ROWB2 + 1024

        def rowb_v(a, b):
            return R5f.v(ROWB + a, ROWB + b)[0:1, :]

        def rowb_k(a, b):
            return R5f.k(ROWB + a, ROWB + b)

        def rowb2_v(a, b):
            return R5f.v(ROWB2 + a, ROWB2 + b)[0:1, :]

        def rowb2_k(a, b):
            return R5f.k(ROWB2 + a, ROWB2 + b)

        def rowt_v(a, b):
            return R5f.v(ROWT + a, ROWT + b)[0:1, :]

        def rowt_k(a, b):
            return R5f.k(ROWT + a, ROWT + b)

        def xb_v(cg, a, b):
            return R5.v(OXB + cg * CHH + a, OXB + cg * CHH + b)

        def xb_k(cg, a, b):
            return R5.k(OXB + cg * CHH + a, OXB + cg * CHH + b)

        def hyb_v(cg, a=0, b=CH):
            return R5.v(OXBC + cg * CH + a, OXBC + cg * CH + b)

        def hyb_k(cg, a=0, b=CH):
            return R5.k(OXBC + cg * CH + a, OXBC + cg * CH + b)

        def xbc_v(cg, a=0, b=CH):
            return R5.v(OXBC + cg * CH + a, OXBC + cg * CH + b)

        def xbc_k(cg, a=0, b=CH):
            return R5.k(OXBC + cg * CH + a, OXBC + cg * CH + b)

        def la_v(cg):
            return R5.v(OLA + cg * CH, OLA + (cg + 1) * CH)

        def la_k(cg):
            return R5.k(OLA + cg * CH, OLA + (cg + 1) * CH)

        def ixb_v(cg):
            return R5.v(OIXB + cg * CH, OIXB + (cg + 1) * CH)

        def ixb_k(cg):
            return R5.k(OIXB + cg * CH, OIXB + (cg + 1) * CH)

        def yb2_v(cg):
            return R5.v(cg * CH, (cg + 1) * CH)

        def yb2_k(cg):
            return R5.k(cg * CH, (cg + 1) * CH)

        def _pj(j):
            return j * CH if j < 8 else OXBC + (j - 8) * CH

        def prod_v(j, a=0, b=CH):
            return R5.v(_pj(j) + a, _pj(j) + b)

        def prod_k(j, a=0, b=CH):
            return R5.k(_pj(j) + a, _pj(j) + b)

        def t_v(cg, a, b):
            return R6.v(cg * CHH + a, cg * CHH + b)

        def t_k(cg, a, b):
            return R6.k(cg * CHH + a, cg * CHH + b)

        def u_v(cg, a=0, b=CH):
            return R6.v(cg * CH + a, cg * CH + b)

        def u_k(cg, a=0, b=CH):
            return R6.k(cg * CH + a, cg * CH + b)

        def K(name, i=0):
            return (f"{name}:{i}",)

        def KP(i):
            return (f"psum:{i}",)

        for name, (sb, dram) in sv.items():
            add("q", (lambda e, d=dram, s=sb: e.dma_start(out=s[:, :], in_=d)),
                writes=K(name), kind="dma", dma_sem="init")
        for _i, _val in enumerate(CONST_VALS):
            add("v", (lambda e, i=_i, val=_val:
                      e.memset(constv[:, i:i + 1], val)), writes=K("consts"))
        add("v", lambda e: e.memset(inv1024[:, :], 1.0 / 1024.0),
            writes=K("inv1024"))
        add("v", lambda e: e.memset(inv1024b[:, :], 1.0 / 1024.0),
            writes=K("inv1024b"))
        add("v", lambda e: e.memset(ones_row[:, :], 1.0), writes=K("ones_row"))
        add("v", lambda e: e.memset(ones_rowb[:, :], 1.0), writes=K("ones_rowb"))

        add("s", lambda e: e.activation(tiny[0][:, :], sv["lam"][0][:, :], AF.Abs),
            reads=K("lam"), writes=K("tiny", 0))
        add("s", lambda e: e.activation(tiny[0][:, :], tiny[0][:, :], AF.Exp, scale=-1.0),
            reads=K("tiny", 0) + K("consts"), writes=K("tiny", 0))
        add("s", lambda e: e.activation(tiny[0][:, :], tiny[0][:, :], AF.Ln, bias=1.0),
            reads=K("tiny", 0) + K("consts"), writes=K("tiny", 0))
        add("s", lambda e: e.activation(tiny[1][:, :], sv["lam"][0][:, :], AF.Relu),
            reads=K("lam"), writes=K("tiny", 1))
        add("v", lambda e: e.tensor_tensor(tiny[0][:, :], tiny[0][:, :], tiny[1][:, :], ALU.add),
            reads=K("tiny", 0) + K("tiny", 1), writes=K("tiny", 0))
        add("v", lambda e: e.tensor_scalar(sv_gneg[:, :], tiny[0][:, :], -8.0, None, ALU.mult),
            reads=K("tiny", 0), writes=K("gneg"))

        st = {"sm": 0, "bg": 0, "ps": 0, "sc": 0}

        def slab_small(wname, mc):
            slot = st["sm"] % len(wsm)
            st["sm"] += 1
            d = wdr[wname]
            add("p", (lambda e, d=d, b=wsm[slot], m=mc:
                      e.dma_start(out=b[:, :], in_=d[m * 128:(m + 1) * 128, :])),
                writes=K("wsm", slot), kind="dma", dma_sem=f"wsm{slot}")
            return slot

        def slab_big(wname, mc, half):
            slot = st["bg"] % len(wbg)
            st["bg"] += 1
            d = wdr[wname]
            add("p", (lambda e, d=d, b=wbg[slot], m=mc, h=half:
                      e.dma_start(out=b[:, :],
                                  in_=d[m * 128:(m + 1) * 128, h * 2048:(h + 1) * 2048])),
                writes=K("wbg", slot), kind="dma", dma_sem=f"wbg{slot}")
            return slot

        st["resv"] = set()

        def ps_next():
            while True:
                i = st["ps"] % 8
                st["ps"] += 1
                if i not in st["resv"]:
                    return i

        def ps_reserve():
            i = ps_next()
            st["resv"].add(i)
            return i

        def ps_release(i):
            st["resv"].discard(i)

        def _mk_scr(i):
            return (lambda a=0, b=CH: scr_t[i][:, a:b],
                    lambda a=0, b=CH: (f"scr{i}:{a // 512}",)
                    if b - a <= 512 else (f"scr{i}:0", f"scr{i}:1"))

        XCF = OXBC // 2
        LAF = OLA // 2
        IXF = OIXB // 2

        def _mk_slot(base, i):
            return (lambda a=0, b=CH: R5f.v(base + i * CH + a, base + i * CH + b),
                    lambda a=0, b=CH: R5f.k(base + i * CH + a, base + i * CH + b))

        SCR_MAIN = [_mk_scr(i) for i in range(3)]

        rg6f = rg6_t.ap().bitcast(F32)

        def _mk_r6(i):
            base = 512 + i * CH
            return (lambda a=0, b=CH: rg6f[:, base + a:base + b],
                    lambda a=0, b=CH: R6.k(2 * (base + a), 2 * (base + b)))
        SCR_R6 = [_mk_r6(i) for i in range(3)]
        SCR_XC = [_mk_slot(XCF, i) for i in range(4)]
        SCR_LA = [_mk_slot(LAF, i) for i in range(4)]
        SCR_IX = [_mk_slot(IXF, i) for i in range(4)]

        endsT = scr_t[0][0:16, 0:128]
        K_endsT = ("scr0:0",)
        gT = scr_t[2][:, 0:128]
        gAll = scr_t[2][:, 0:128]
        K_gTall = ("scr2:0",)
        fsc = [scr_t[0][:, 0:128], scr_t[1][:, 0:128]]
        K_fsc = [("scr0:0",), ("scr1:0",)]

        pool_state = {"pool": SCR_MAIN, "i": 0}

        def set_pool(p):
            pool_state["pool"] = p
            pool_state["i"] = 0

        def scr_next():
            p = pool_state["pool"]
            i = pool_state["i"] % len(p)
            pool_state["i"] += 1
            return p[i]

        def mm_group(wkey, wbuf, nk, rhs_v, rhs_k, groups, pss):
            for ki in range(nk):
                lhs = wbuf[:, ki * 128:(ki + 1) * 128]
                for gi, (c0, c1) in enumerate(groups):
                    pi = pss[gi]
                    add("pe",
                        (lambda e, p=ps[pi], l=lhs, r=rhs_v(ki, c0, c1),
                         w=c1 - c0, stt=(ki == 0), sp=(ki == nk - 1):
                         e.matmul(p[:, 0:w], l, r, start=stt, stop=sp)),
                        reads=wkey + rhs_k(ki, c0, c1),
                        writes=KP(pi))

        def norm_stats_begin(want_mean):
            stn = {"sq": [ps_reserve() for _ in G_MAIN]}
            if want_mean:
                stn["mn"] = [ps_reserve() for _ in G_MAIN]
            return stn

        def norm_stats_cg(stn, cg, src_v, src_k, first, last):
            for gi, (c0, c1) in enumerate(G_MAIN):
                w = c1 - c0
                sv_, sk_ = scr_next()
                sqb = sv_(0, w).bitcast(BF16)[:, 0:w]
                add("s", (lambda e, sc=sqb, sr=src_v(cg, c0, c1):
                          e.activation(sc, sr, AF.Square)),
                    reads=src_k(cg, c0, c1), writes=sk_(0, w))
                add("pe", (lambda e, p=ps[stn["sq"][gi]], sc=sqb,
                           stt=first, sp=last, w=w:
                           e.matmul(p[0:1, 0:w], inv1024b[:, :], sc,
                                    start=stt, stop=sp)),
                    reads=sk_(0, w) + K("inv1024b"), writes=KP(stn["sq"][gi]))
                if "mn" in stn:
                    add("pe", (lambda e, p=ps[stn["mn"][gi]],
                               sr=src_v(cg, c0, c1), stt=first, sp=last, w=w:
                               e.matmul(p[0:1, 0:w], inv1024[:, :], sr,
                                        start=stt, stop=sp)),
                        reads=src_k(cg, c0, c1) + K("inv1024"),
                        writes=KP(stn["mn"][gi]))

        def rms_finish(stn, src_v, src_k, wvec, wkey, dst_v, dst_k, eps):
            for gi, (c0, c1) in enumerate(G_MAIN):
                w = c1 - c0
                pstat = stn["sq"][gi]
                add("s", (lambda e, p=ps[pstat], r=rowb_v(c0, c1), w=w:
                          e.activation(r, p[0:1, 0:w], AF.Ln, bias=eps)),
                    reads=KP(pstat) + K("consts"), writes=rowb_k(c0, c1))
                rb16 = rowt_v(c0, c0 + w // 2).bitcast(BF16)[0:1, 0:w]
                add("s", (lambda e, r=rowb_v(c0, c1), rb=rb16:
                          e.activation(rb, r, AF.Exp, scale=-0.5)),
                    reads=rowb_k(c0, c1) + K("consts"),
                    writes=rowt_k(c0, c0 + w // 2))
                pbc = ps_next()
                add("pe", (lambda e, p=ps[pbc], r=rb16, w=w:
                           e.matmul(p[:, 0:w], ones_rowb[:, :],
                                    r, start=True, stop=True)),
                    reads=rowt_k(c0, c0 + w // 2) + K("ones_rowb"),
                    writes=KP(pbc))
                for cg in range(NCG):
                    add("v", (lambda e, p=ps[pbc], sr=src_v(cg, c0, c1),
                              d=dst_v(cg, c0, c1), wv=wvec[:, cg:cg + 1], w=w:
                              e.scalar_tensor_tensor(d, sr, wv, p[:, 0:w],
                                                     ALU.mult, ALU.mult)),
                        reads=src_k(cg, c0, c1) + KP(pbc) + wkey,
                        writes=dst_k(cg, c0, c1))
                ps_release(pstat)

        def ln_finish(stn, src_v, src_k, dst_v, dst_k, eps, extra=None,
                      stream_out=None):
            bcs = []
            for gi, (c0, c1) in enumerate(G_MAIN):
                w = c1 - c0
                psq = stn["sq"][gi]
                psm = stn["mn"][gi]
                add("s", (lambda e, p=ps[psm], r=rowt_v(c0, c0 + w), w=w:
                          e.activation(r, p[0:1, 0:w], AF.Square)),
                    reads=KP(psm), writes=rowt_k(c0, c1))
                rb2m = rowb2_v(c0, c1).bitcast(BF16)[0:1, 0:w]
                add("s", (lambda e, p=ps[psm], r=rb2m, w=w:
                          e.copy(r, p[0:1, 0:w])),
                    reads=KP(psm), writes=rowb2_k(c0, c1))
                add("v", (lambda e, p=ps[psq], r=rowb_v(c0, c1),
                          rt=rowt_v(c0, c1), w=w:
                          e.tensor_tensor(r, p[0:1, 0:w], rt, ALU.subtract)),
                    reads=KP(psq) + rowt_k(c0, c1), writes=rowb_k(c0, c1))
                add("s", (lambda e, r=rowb_v(c0, c1):
                          e.activation(r, r, AF.Ln, bias=eps)),
                    reads=rowb_k(c0, c1) + K("consts"), writes=rowb_k(c0, c1))
                rb16 = rowt_v(c0, c0 + w // 2).bitcast(BF16)[0:1, 0:w]
                add("s", (lambda e, r=rowb_v(c0, c1), rb=rb16:
                          e.activation(rb, r, AF.Exp, scale=-0.5)),
                    reads=rowb_k(c0, c1) + K("consts"),
                    writes=rowt_k(c0, c0 + w // 2))
                rbp = rowb_v(c0, c1).bitcast(BF16)[0:1, 0:w]
                add("v", (lambda e, r=rb16, r2=rb2m, d=rbp:
                          e.tensor_tensor(d, r2, r, ALU.mult)),
                    reads=rowt_k(c0, c0 + w // 2) + rowb2_k(c0, c1),
                    writes=rowb_k(c0, c1))
                pb1 = ps_next()
                pb2 = ps_next()
                add("pe", (lambda e, p=ps[pb1], r=rb16, w=w:
                           e.matmul(p[:, 0:w], ones_rowb[:, :], r,
                                    start=True, stop=True)),
                    reads=rowt_k(c0, c0 + w // 2) + K("ones_rowb"),
                    writes=KP(pb1))
                add("pe", (lambda e, p=ps[pb2], r=rbp, w=w:
                           e.matmul(p[:, 0:w], ones_rowb[:, :], r,
                                    start=True, stop=True)),
                    reads=rowb_k(c0, c1) + K("ones_rowb"), writes=KP(pb2))
                bcs.append((pb1, pb2))
                ps_release(psq)
                ps_release(psm)
            sweep = NCG if stream_out is None else 4
            for base in range(0, NCG, sweep):
                cgs = list(range(base, base + sweep))
                for gi, (c0, c1) in enumerate(G_MAIN):
                    pb1, _ = bcs[gi]
                    w = c1 - c0
                    for cg in cgs:
                        add("v", (lambda e, p=ps[pb1], sr=src_v(cg, c0, c1),
                                  d=dst_v(cg, c0, c1), w=w:
                                  e.tensor_tensor(d, sr, p[:, 0:w], ALU.mult)),
                            reads=src_k(cg, c0, c1) + KP(pb1),
                            writes=dst_k(cg, c0, c1))
                for gi, (c0, c1) in enumerate(G_MAIN):
                    _, pb2 = bcs[gi]
                    w = c1 - c0
                    for cg in cgs:
                        add("v", (lambda e, p=ps[pb2], d=dst_v(cg, c0, c1), w=w:
                                  e.tensor_tensor(d, d, p[:, 0:w], ALU.subtract)),
                            reads=dst_k(cg, c0, c1) + KP(pb2),
                            writes=dst_k(cg, c0, c1))
                for cg in cgs:
                    if extra is not None:
                        ev, ek = extra
                        for (c0, c1) in G_MAIN:
                            add("s", (lambda e, d=dst_v(cg, c0, c1),
                                      g=ev(cg, c0, c1):
                                      e.copy(g, d)),
                                reads=dst_k(cg, c0, c1), writes=ek(cg, c0, c1))
                    if stream_out is not None:
                        add("q", (lambda e, cg=cg, d=stream_out:
                                  e.dma_start(out=d[cg * 128:(cg + 1) * 128, :],
                                              in_=dst_v(cg, 0, CH))),
                            reads=dst_k(cg, 0, CH), kind="dma",
                            dma_sem=f"y{cg % 2}")
        def rms_norm(src_v, src_k, wvec, wkey, dst_v, dst_k, groups, eps):
            for (c0, c1) in groups:
                w = c1 - c0
                pstat = ps_next()
                for cg in range(NCG):
                    sv_, sk_ = scr_next()
                    sqb = sv_(0, w).bitcast(BF16)[:, 0:w]
                    add("s", (lambda e, sc=sqb, s=src_v(cg, c0, c1):
                              e.activation(sc, s, AF.Square)),
                        reads=src_k(cg, c0, c1), writes=sk_(0, w))
                    add("pe", (lambda e, p=ps[pstat], sc=sqb,
                               stt=(cg == 0), sp=(cg == NCG - 1), w=w:
                               e.matmul(p[0:1, 0:w], inv1024b[:, :], sc,
                                        start=stt, stop=sp)),
                        reads=sk_(0, w) + K("inv1024b"), writes=KP(pstat))
                add("s", (lambda e, p=ps[pstat], r=rowb_v(c0, c1), w=w:
                          e.activation(r, p[0:1, 0:w], AF.Ln, bias=eps)),
                    reads=KP(pstat) + K("consts"), writes=rowb_k(c0, c1))
                add("s", (lambda e, r=rowb_v(c0, c1):
                          e.activation(r, r, AF.Exp, scale=-0.5)),
                    reads=rowb_k(c0, c1) + K("consts"), writes=rowb_k(c0, c1))
                pbc = ps_next()
                add("pe", (lambda e, p=ps[pbc], r=rowb_v(c0, c1), w=w:
                           e.matmul(p[:, 0:w], ones_row[:, :],
                                    r, start=True, stop=True)),
                    reads=rowb_k(c0, c1) + K("ones_row"), writes=KP(pbc))
                for cg in range(NCG):
                    add("v", (lambda e, p=ps[pbc], s=src_v(cg, c0, c1),
                              d=dst_v(cg, c0, c1), wv=wvec[:, cg:cg + 1], w=w:
                              e.scalar_tensor_tensor(d, s, wv, p[:, 0:w],
                                                     ALU.mult, ALU.mult)),
                        reads=src_k(cg, c0, c1) + KP(pbc) + wkey,
                        writes=dst_k(cg, c0, c1))

        def layer_norm(src_v, src_k, dst_v, dst_k, eps, extra=None,
                       stream_out=None, src2_v=None, src2_k=None):
            if src2_v is not None:
                src_v, src_k = src2_v, src2_k
            bcs = []
            for (c0, c1) in G_MAIN:
                w = c1 - c0
                psq = ps_next()
                psm = ps_next()
                for cg in range(NCG):
                    sv_, sk_ = scr_next()
                    sqb = sv_(0, w).bitcast(BF16)[:, 0:w]
                    add("s", (lambda e, sc=sqb, s=src_v(cg, c0, c1):
                              e.activation(sc, s, AF.Square)),
                        reads=src_k(cg, c0, c1), writes=sk_(0, w))
                    add("pe", (lambda e, p=ps[psq], sc=sqb,
                               stt=(cg == 0), sp=(cg == NCG - 1), w=w:
                               e.matmul(p[0:1, 0:w], inv1024b[:, :], sc,
                                        start=stt, stop=sp)),
                        reads=sk_(0, w) + K("inv1024b"), writes=KP(psq))
                    add("pe", (lambda e, p=ps[psm], s=src_v(cg, c0, c1),
                               stt=(cg == 0), sp=(cg == NCG - 1), w=w:
                               e.matmul(p[0:1, 0:w], inv1024[:, :], s,
                                        start=stt, stop=sp)),
                        reads=src_k(cg, c0, c1) + K("inv1024"), writes=KP(psm))
                add("s", (lambda e, p=ps[psm], r=rowt_v(c0, c0 + w), w=w:
                          e.activation(r, p[0:1, 0:w], AF.Square)),
                    reads=KP(psm), writes=rowt_k(c0, c1))
                rb2m = rowb2_v(c0, c1).bitcast(BF16)[0:1, 0:w]
                add("s", (lambda e, p=ps[psm], r=rb2m, w=w:
                          e.copy(r, p[0:1, 0:w])),
                    reads=KP(psm), writes=rowb2_k(c0, c1))
                add("v", (lambda e, p=ps[psq], r=rowb_v(c0, c1),
                          rt=rowt_v(c0, c1), w=w:
                          e.tensor_tensor(r, p[0:1, 0:w], rt, ALU.subtract)),
                    reads=KP(psq) + rowt_k(c0, c1), writes=rowb_k(c0, c1))
                add("s", (lambda e, r=rowb_v(c0, c1):
                          e.activation(r, r, AF.Ln, bias=eps)),
                    reads=rowb_k(c0, c1) + K("consts"), writes=rowb_k(c0, c1))
                rb16 = rowt_v(c0, c0 + w // 2).bitcast(BF16)[0:1, 0:w]
                add("s", (lambda e, r=rowb_v(c0, c1), rb=rb16:
                          e.activation(rb, r, AF.Exp, scale=-0.5)),
                    reads=rowb_k(c0, c1) + K("consts"),
                    writes=rowt_k(c0, c0 + w // 2))
                rbp = rowb_v(c0, c1).bitcast(BF16)[0:1, 0:w]
                add("v", (lambda e, r=rb16, r2=rb2m, d=rbp:
                          e.tensor_tensor(d, r2, r, ALU.mult)),
                    reads=rowt_k(c0, c0 + w // 2) + rowb2_k(c0, c1),
                    writes=rowb_k(c0, c1))
                pb1 = ps_next()
                pb2 = ps_next()
                add("pe", (lambda e, p=ps[pb1], r=rb16, w=w:
                           e.matmul(p[:, 0:w], ones_rowb[:, :], r,
                                    start=True, stop=True)),
                    reads=rowt_k(c0, c0 + w // 2) + K("ones_rowb"),
                    writes=KP(pb1))
                add("pe", (lambda e, p=ps[pb2], r=rbp, w=w:
                           e.matmul(p[:, 0:w], ones_rowb[:, :], r,
                                    start=True, stop=True)),
                    reads=rowb_k(c0, c1) + K("ones_rowb"), writes=KP(pb2))
                bcs.append((pb1, pb2))
            sweep = NCG if stream_out is None else 4
            for base in range(0, NCG, sweep):
                cgs = list(range(base, base + sweep))
                for gi, (c0, c1) in enumerate(G_MAIN):
                    pb1, _ = bcs[gi]
                    w = c1 - c0
                    for cg in cgs:
                        add("v", (lambda e, p=ps[pb1], s=src_v(cg, c0, c1),
                                  d=dst_v(cg, c0, c1), w=w:
                                  e.tensor_tensor(d, s, p[:, 0:w], ALU.mult)),
                            reads=src_k(cg, c0, c1) + KP(pb1),
                            writes=dst_k(cg, c0, c1))
                for gi, (c0, c1) in enumerate(G_MAIN):
                    _, pb2 = bcs[gi]
                    w = c1 - c0
                    for cg in cgs:
                        add("v", (lambda e, p=ps[pb2], d=dst_v(cg, c0, c1), w=w:
                                  e.tensor_tensor(d, d, p[:, 0:w], ALU.subtract)),
                            reads=dst_k(cg, c0, c1) + KP(pb2),
                            writes=dst_k(cg, c0, c1))
                for cg in cgs:
                    if extra is not None:
                        ev, ek = extra
                        for (c0, c1) in G_MAIN:
                            add("s", (lambda e, d=dst_v(cg, c0, c1),
                                      g=ev(cg, c0, c1):
                                      e.copy(g, d)),
                                reads=dst_k(cg, c0, c1), writes=ek(cg, c0, c1))
                    if stream_out is not None:
                        add("q", (lambda e, cg=cg, d=stream_out:
                                  e.dma_start(out=d[cg * 128:(cg + 1) * 128, :],
                                              in_=dst_v(cg, 0, CH))),
                            reads=dst_k(cg, 0, CH), kind="dma",
                            dma_sem=f"y{cg % 2}")

        def pass_head(pi):
            x_dram = xs[pi]
            add("q", (lambda e, d=x_dram:
                      e.dma_start(
                          out=rg1_t.ap().rearrange("p (g n) -> p g n", g=NCG, n=CHH),
                          in_=d.rearrange("(g p) n -> p g n", p=128, g=NCG))),
                writes=R1.kall(), kind="dma", dma_sem="x")
            set_pool(SCR_MAIN)
            rms_norm(x_v, x_k, sv["norm1w"][0], K("norm1w"),
                     t_v, t_k, G_HALO, EPS_RMS)

        def do_pass(pi, head_next=None):
            x_dram = xs[pi]

            for mc in range(NCG):
                slot = slab_small("wx", mc)
                p3 = [ps_next(), ps_next(), ps_next()]
                mm_group(K("wsm", slot), wsm[slot], NCG, t_v, t_k, G_HALO, p3)
                for gi, (c0, c1) in enumerate(G_HALO):
                    add("s", (lambda e, p=ps[p3[gi]], d=xb_v(mc, c0, c1), w=c1 - c0:
                              e.copy(d, p[:, 0:w])),
                        reads=KP(p3[gi]),
                        writes=tuple(k + f":g{gi}" for k in xb_k(mc, c0, c1)))

            accs = SCR_LA + SCR_IX
            wy_state = {"m": 0}

            def wy_group():
                mc = wy_state["m"]
                wy_state["m"] += 1
                slot = slab_small("wy", mc)
                p2 = [ps_next(), ps_next()]
                mm_group(K("wsm", slot), wsm[slot], NCG, t_v, t_k,
                         [(3, 515), (515, 1027)], p2)
                for gi, (c0, c1) in enumerate(G_MAIN):
                    add("s", (lambda e, p=ps[p2[gi]], d=yb2_v(mc)[:, c0:c1],
                              w=c1 - c0:
                              e.copy(d, p[:, 0:w])),
                        reads=KP(p2[gi]), writes=yb2_k(mc))

            def conv_sweep(k):
                for cg in range(NCG):
                    xbread = xb_k(cg, 0, CHH) + tuple(
                        k + f":g{gi}" for gi in range(3)
                        for k in xb_k(cg, 0, CHH))
                    if k == 0:
                        add("v", (lambda e, sc=xbc_v(cg), s=xb_v(cg, 0, CH),
                                  w0=sv["convw"][0][:, cg:cg + 1],
                                  cb=sv["convb"][0][:, cg:cg + 1]:
                                  e.tensor_scalar(sc, s, w0, cb, ALU.mult, ALU.add)),
                            reads=xbread + K("convw") + K("convb"),
                            writes=xbc_k(cg))
                    else:
                        add("v", (lambda e, sc=xbc_v(cg), s=xb_v(cg, k, CH + k),
                                  wk=sv["convw"][0][:, k * NCG + cg:k * NCG + cg + 1]:
                                  e.scalar_tensor_tensor(sc, s, wk, sc,
                                                         ALU.mult, ALU.add)),
                            reads=xbread + K("convw") + xbc_k(cg),
                            writes=xbc_k(cg))

            for k in range(4):
                conv_sweep(k)
                wy_group()
                wy_group()

            add("q", (lambda e, d=x_dram:
                      e.dma_start(
                          out=rg1_t.ap().rearrange("p (g n) -> p g n",
                                                   g=NCG, n=CHH)[:, :, 0:CH],
                          in_=d.rearrange("(g p) n -> p g n",
                                          p=128, g=NCG)[:, :, 3:CHH])),
                writes=R1.kall(), kind="dma", dma_sem="x")

            set_pool(SCR_MAIN)

            def gate_group(wname, mc, out_emit):
                slot = slab_small(wname, mc)
                p2 = [ps_next(), ps_next()]
                mm_group(K("wsm", slot), wsm[slot], NCG,
                         lambda k, a, b: xbc_v(k, a, b),
                         lambda k, a, b: xbc_k(k, a, b), G_MAIN, p2)
                sv_, sk_ = scr_next()
                sb16 = sv_(0, 512).bitcast(BF16)
                for gi, (c0, c1) in enumerate(G_MAIN):
                    add("s", (lambda e, p=ps[p2[gi]], sc=sb16[:, c0:c1], w=c1 - c0:
                              e.activation(sc, p[:, 0:w], AF.Sigmoid)),
                        reads=KP(p2[gi]), writes=sk_(0, 512))
                out_emit(sb16, lambda: sk_(0, 512))

            def wa_out(mc):
                def go(sb16, skf):
                    add("v", (lambda e, sc=sb16, d=la_v(mc),
                              g=sv_gneg[:, mc:mc + 1]:
                              e.tensor_scalar(d, sc, g, None, ALU.mult)),
                        reads=skf() + K("gneg"), writes=la_k(mc))
                return go

            def wi_out(mc):
                def go(sb16, skf):
                    add("v", (lambda e, sc=sb16, d=ixb_v(mc), x=xbc_v(mc):
                              e.tensor_tensor(d, sc, x, ALU.mult)),
                        reads=skf() + xbc_k(mc), writes=ixb_k(mc))
                return go

            def gelu_unit(g_v, g_k):
                tAf, tAk = scr_next()
                tBf, tBk = scr_next()
                tA = tAf(0, 512).bitcast(BF16)
                tB = tBf(0, 512).bitcast(BF16)
                kA = lambda a=0, b=CH: tAk(0, 512)
                kB = lambda a=0, b=CH: tBk(0, 512)
                add("s", (lambda e, sc=tA, g=g_v: e.activation(sc, g, AF.Square)),
                    reads=g_k, writes=kA())
                add("s", (lambda e, sc=tA, sd=tB:
                          e.activation(sd, sc, AF.Identity,
                                       scale=2.0 * GC * GA, bias=2.0 * GC)),
                    reads=kA() + K("consts"), writes=kB())
                add("v", (lambda e, sc=tA, sd=tB, g=g_v:
                          e.tensor_tensor(sc, g, sd, ALU.mult)),
                    reads=g_k + kB(), writes=kA())
                add("s", (lambda e, sc=tA, sd=tB:
                          e.activation(sd, sc, AF.Sigmoid)),
                    reads=kA(), writes=kB())
                return (lambda a=0, b=CH: tA[:, a:b]), kA, (lambda a=0, b=CH: tB[:, a:b]), kB

            for mc in range(NCG):
                gate_group("wa", mc, wa_out(mc))

            def sig_half(mcs):
                for step in range(len(mcs) // 2):
                    gate_group("wi", mcs[2 * step], wi_out(mcs[2 * step]))
                    gate_group("wi", mcs[2 * step + 1], wi_out(mcs[2 * step + 1]))
                    for mc in (mcs[2 * step], mcs[2 * step + 1]):
                        ykall = yb2_k(mc) + tuple(
                            k + f":h{gi}" for gi in range(2)
                            for k in yb2_k(mc))
                        tA, tAk, tB, tBk = gelu_unit(yb2_v(mc), ykall)
                        add("v", (lambda e, g=yb2_v(mc), sd=tB():
                                  e.tensor_tensor(g, g, sd, ALU.mult)),
                            reads=ykall + tBk(), writes=ykall)

            sig_half([0, 1, 2, 3])
            set_pool(SCR_MAIN + SCR_R6)
            add("v", (lambda e: e.memset(rg6_t[:, 0:CH], 0.0)),
                reads=(), writes=R6.k(0, CH))

            def d3_pair(mcs):
                u = {}
                for mc in mcs:
                    tiles = []
                    for _ in range(3):
                        tv, tk = scr_next()
                        tiles.append(((lambda a=0, b=CH, tv=tv:
                                       tv(0, CH).bitcast(BF16)[:, a:b]), tk))
                    u[mc] = tuple(tiles)
                for mc in mcs:
                    add("v", (lambda e, l=la_v(mc), mc=mc:
                              e.reduce_sum(ends[:, mc:mc + 1], l,
                                           axis=mybir.AxisListType.X)),
                        reads=la_k(mc), writes=(f"ends:c{mc}",))
                for mc in mcs:
                    (sa, sak), _, _ = u[mc]
                    add("s", (lambda e, sc=sa(), l=la_v(mc):
                              e.activation(sc, l, AF.Exp)),
                        reads=la_k(mc), writes=sak())
                for mc in mcs:
                    (sa, sak), (sb2, sb2k), _ = u[mc]
                    add("v", (lambda e, sa_=sa(), sc=sb2():
                              e.tensor_tensor(sc, sa_, sa_, ALU.mult)),
                        reads=sak() + sb2k(), writes=sb2k())
                for mc in mcs:
                    _, (sb2, sb2k), _ = u[mc]
                    add("v", (lambda e, sc=sb2():
                              e.tensor_scalar(sc, sc, -1.0, 1.0, ALU.mult, ALU.add)),
                        reads=sb2k(), writes=sb2k())
                for mc in mcs:
                    _, (sb2, sb2k), _ = u[mc]
                    add("s", (lambda e, sc=sb2(): e.activation(sc, sc, AF.Ln)),
                        reads=sb2k(), writes=sb2k())
                for mc in mcs:
                    _, (sb2, sb2k), _ = u[mc]
                    add("s", (lambda e, sc=sb2():
                              e.activation(sc, sc, AF.Exp, scale=0.5)),
                        reads=sb2k() + K("consts"), writes=sb2k())
                for mc in mcs:
                    _, (sb2, sb2k), (sbb, sbbk) = u[mc]
                    add("v", (lambda e, sc=sb2(), sd=sbb(), i=ixb_v(mc):
                              e.scalar_tensor_tensor(sd, sc, 1e-12, i,
                                                     ALU.max, ALU.mult)),
                        reads=sb2k() + ixb_k(mc), writes=sbbk())
                for mc in mcs:
                    (sa, sak), _, (sbb, sbbk) = u[mc]
                    add("v", (lambda e, sc=sa(), sd=sbb(), d=h_v(mc):
                              e.tensor_tensor_scan(d, sc, sd, 0.0,
                                                   ALU.mult, ALU.add)),
                        reads=sak() + sbbk(), writes=h_k(mc))

            d3_pair([0, 1])
            d3_pair([2, 3])
            sig_half([4, 5, 6, 7])
            set_pool(SCR_MAIN + SCR_R6 + SCR_XC)
            d3_pair([4, 5])
            d3_pair([6, 7])

            add("s", (lambda e: e.activation(ends[:, 0:8], ends[:, 0:8], AF.Exp)),
                reads=tuple(f"ends:c{i}" for i in range(8)),
                writes=tuple(f"ends:c{i}" for i in range(8)))
            for mc in range(NCG):
                add("v", (lambda e, mc=mc, h=h_v(mc):
                          e.tensor_copy(ends[:, 8 + mc:9 + mc], h[:, CH - 1:CH])),
                    reads=h_k(mc), writes=(f"ends:c{8 + mc}",))
            pT = ps_next()
            add("pe", (lambda e, p=ps[pT]:
                       e.transpose(p[0:16, 0:128], ends[:, :], sv["ident"][0][:, :])),
                reads=tuple(f"ends:c{i}" for i in range(16)) + K("ident"),
                writes=KP(pT))
            add("s", (lambda e, p=ps[pT]: e.copy(endsT[:, :], p[0:16, 0:128])),
                reads=KP(pT), writes=K_endsT)
            add("q", (lambda e, d=ag_in[pi]: e.dma_start(out=d.ap(), in_=endsT[:, :])),
                reads=K_endsT, writes=K(f"agin{pi}"), kind="dma",
                dma_sem=f"ag{pi}")
            cop = add("p", (lambda e, i=pi:
                            e.collective_compute(
                                "AllGather", ALU.bypass,
                                replica_groups=[list(range(N_CORES))],
                                ins=[ag_in[i].ap().opt()],
                                outs=[ag_out[i].ap().opt()])),
                      reads=K(f"agin{pi}"), writes=K(f"agout{pi}"), kind="coll")
            cop.dma_sem = cc[pi]
            cop.dma_val = 1
            for mc in range(NCG):
                sa2, sa2k = scr_next()
                add("s", (lambda e, sc=sa2(), l=la_v(mc):
                          e.activation(sc, l, AF.Exp)),
                    reads=la_k(mc), writes=sa2k())
                add("v", (lambda e, sc=sa2(), d=A_v(mc):
                          e.tensor_tensor_scan(d, sc, rg6_t[:, 0:CH], 1.0,
                                               ALU.mult, ALU.add)),
                    reads=sa2k() + R6.k(0, CH), writes=A_k(mc))
            add("q", (lambda e, d=ag_out[pi]: e.dma_start(out=gT[:, :], in_=d.ap())),
                reads=K(f"agout{pi}"), writes=K_gTall, kind="dma",
                dma_sem=f"ag{pi}")
            pT2 = ps_next()
            add("pe", (lambda e, p=ps[pT2]:
                       e.transpose(p[:, 0:128], gT[:, :], sv["ident"][0][:, :])),
                reads=K_gTall + K("ident"), writes=KP(pT2))
            add("s", (lambda e, p=ps[pT2]: e.copy(gAll[:, :], p[:, 0:128])),
                reads=KP(pT2), writes=K_gTall)

            def fold(mname, dstt, dkey):
                add("v", (lambda e, m=sv[mname][0]: e.tensor_tensor(
                    fsc[0][:, :], gAll[:, :], m[:, :], ALU.mult)),
                    reads=K_gTall + K(mname), writes=K_fsc[0])
                add("v", (lambda e: e.tensor_tensor(
                    fsc[1][:, 0:64], fsc[0][:, 0:64], fsc[0][:, 64:128], ALU.add)),
                    reads=K_fsc[0], writes=K_fsc[1])
                add("v", (lambda e: e.tensor_tensor(
                    fsc[0][:, 0:32], fsc[1][:, 0:32], fsc[1][:, 32:64], ALU.add)),
                    reads=K_fsc[1], writes=K_fsc[0])
                add("v", (lambda e, d=dstt: e.tensor_tensor(
                    d[:, 0:16], fsc[0][:, 0:16], fsc[0][:, 16:32], ALU.add)),
                    reads=K_fsc[0], writes=dkey)

            if pi == 0:
                fold("mask_e", foldE, K("foldE"))
                fold("mask_o", foldO, K("foldO"))
                add("v", (lambda e: e.tensor_tensor(
                    tiny[0][:, :], foldO[:, 0:8], foldE[:, 8:16], ALU.mult)),
                    reads=K("foldO") + K("foldE"), writes=K("tiny", 0))
                add("v", (lambda e: e.tensor_tensor(
                    H2[:, :], tiny[0][:, :], foldO[:, 8:16], ALU.add)),
                    reads=K("tiny", 0) + K("foldO"), writes=K("H2"))
                add("v", (lambda e: e.tensor_scalar(
                    hin[:, :], foldE[:, 8:16], sv["slf"][0][:, 0:1], None, ALU.mult)),
                    reads=K("foldE") + K("slf"), writes=K("hin"))
            else:
                fold("mask_e", foldE, K("foldE"))
                add("v", (lambda e: e.tensor_tensor(
                    tiny[0][:, :], foldE[:, 0:8], H2[:, :], ALU.mult)),
                    reads=K("foldE") + K("H2"), writes=K("tiny", 0))
                add("v", (lambda e: e.tensor_tensor(
                    tiny[0][:, :], tiny[0][:, :], foldE[:, 8:16], ALU.add)),
                    reads=K("tiny", 0) + K("foldE"), writes=K("tiny", 0))
                add("v", (lambda e: e.tensor_scalar(
                    tiny[1][:, :], H2[:, :], sv["slf"][0][:, 1:2], None, ALU.mult)),
                    reads=K("H2") + K("slf"), writes=K("tiny", 1))
                add("v", (lambda e: e.tensor_scalar(
                    tiny[0][:, :], tiny[0][:, :], sv["slf"][0][:, 0:1], None, ALU.mult)),
                    reads=K("tiny", 0) + K("slf"), writes=K("tiny", 0))
                add("v", (lambda e: e.tensor_tensor(
                    hin[:, :], tiny[0][:, :], tiny[1][:, :], ALU.add)),
                    reads=K("tiny", 0) + K("tiny", 1), writes=K("hin"))

            for mc in range(NCG):
                add("v", (lambda e, a=A_v(mc), d=h_v(mc), s=hin[:, mc:mc + 1]:
                          e.scalar_tensor_tensor(d, a, s, d, ALU.mult, ALU.add)),
                    reads=A_k(mc) + K("hin") + h_k(mc), writes=h_k(mc))
            for mc in range(NCG):
                add("v", (lambda e, d=hyb_v(mc), h=h_v(mc), y=yb2_v(mc):
                          e.tensor_tensor(d, h, y, ALU.mult)),
                    reads=h_k(mc) + yb2_k(mc), writes=hyb_k(mc))

            set_pool(SCR_MAIN)
            rms2_st = norm_stats_begin(False)
            for mc in range(NCG):
                slot = slab_small("wo", mc)
                p2 = [ps_next(), ps_next()]
                mm_group(K("wsm", slot), wsm[slot], NCG,
                         lambda k, a, b: hyb_v(k, a, b),
                         lambda k, a, b: hyb_k(k, a, b), G_MAIN, p2)
                for gi, (c0, c1) in enumerate(G_MAIN):
                    add("v", (lambda e, p=ps[p2[gi]], x=x_v(mc, c0, c1),
                              d=h_v(mc, c0, c1), w=c1 - c0:
                              e.tensor_tensor(d, p[:, 0:w], x, ALU.add)),
                        reads=KP(p2[gi]) + x_k(mc, c0, c1),
                        writes=h_k(mc, c0, c1))
                add("v", (lambda e, d=ln1_v(mc), h=h_v(mc), x=x_v(mc, 0, CH):
                          e.tensor_tensor(d, h, x, ALU.add)),
                    reads=h_k(mc) + x_k(mc, 0, CH), writes=ln1_k(mc))
            for mc in range(NCG):
                norm_stats_cg(rms2_st, mc,
                              lambda cg, a, b: h_v(cg, a, b),
                              lambda cg, a, b: h_k(cg, a, b),
                              mc == 0, mc == NCG - 1)

            rms_finish(rms2_st,
                       lambda cg, a, b: h_v(cg, a, b),
                       lambda cg, a, b: h_k(cg, a, b),
                       sv["norm2w"][0], K("norm2w"),
                       lambda cg, a, b: u_v(cg, a, b),
                       lambda cg, a, b: u_k(cg, a, b), EPS_RMS)

            set_pool(SCR_MAIN + SCR_LA)
            for half in range(2):
                for j in range(16):
                    m2 = half * 16 + j
                    slot_g = slab_small("wg", m2)
                    pg = [ps_next(), ps_next()]
                    mm_group(K("wsm", slot_g), wsm[slot_g], NCG,
                             lambda k, a, b: u_v(k, a, b),
                             lambda k, a, b: u_k(k, a, b), G_MAIN, pg)
                    slot_u = slab_small("wu", m2)
                    pu = [ps_next(), ps_next()]
                    mm_group(K("wsm", slot_u), wsm[slot_u], NCG,
                             lambda k, a, b: u_v(k, a, b),
                             lambda k, a, b: u_k(k, a, b), G_MAIN, pu)
                    for gi, (c0, c1) in enumerate(G_MAIN):
                        add("s", (lambda e, p=ps[pg[gi]],
                                  d=prod_v(j, c0, c1), w=c1 - c0:
                                  e.copy(d, p[:, 0:w])),
                            reads=KP(pg[gi]), writes=prod_k(j, c0, c1))
                    tA, tAk, tB, tBk = gelu_unit(prod_v(j), prod_k(j))
                    add("v", (lambda e, sc=tA(), sd=tB(), g=prod_v(j):
                              e.tensor_tensor(sc, g, sd, ALU.mult)),
                        reads=prod_k(j) + tBk(), writes=tAk())
                    for gi, (c0, c1) in enumerate(G_MAIN):
                        add("v", (lambda e, p=ps[pu[gi]], sc=tA(c0, c1),
                                  d=prod_v(j, c0, c1), w=c1 - c0:
                                  e.tensor_tensor(d, sc, p[:, 0:w], ALU.mult)),
                            reads=KP(pu[gi]) + tAk(c0, c1),
                            writes=prod_k(j, c0, c1))

                if half == 1:
                    set_pool(SCR_MAIN)
                    ln1_st = norm_stats_begin(True)
                for md in range(NCG):
                    slot = slab_big("wd", md, half)
                    p2 = [ps_next(), ps_next()]
                    mm_group(K("wbg", slot), wbg[slot], 16,
                             lambda k, a, b: prod_v(k, a, b),
                             lambda k, a, b: prod_k(k, a, b), G_MAIN, p2)
                    for gi, (c0, c1) in enumerate(G_MAIN):
                        if half == 0:
                            add("v", (lambda e, p=ps[p2[gi]],
                                      d=x_v(md, c0, c1), w=c1 - c0:
                                      e.tensor_copy(d, p[:, 0:w])),
                                reads=KP(p2[gi]), writes=x_k(md, c0, c1))
                        else:
                            add("v", (lambda e, p=ps[p2[gi]],
                                      d=x_v(md, c0, c1), w=c1 - c0:
                                      e.tensor_tensor(d, p[:, 0:w], d, ALU.add)),
                                reads=KP(p2[gi]) + x_k(md, c0, c1),
                                writes=x_k(md, c0, c1))
                    if half == 1:
                        add("v", (lambda e, d=ln1_v(md), wv=x_v(md, 0, CH):
                                  e.tensor_tensor(d, wv, d, ALU.add)),
                            reads=x_k(md, 0, CH) + ln1_k(md), writes=ln1_k(md))
            for md in range(NCG):
                norm_stats_cg(ln1_st, md,
                              lambda cg, a, b: ln1_v(cg, a, b),
                              lambda cg, a, b: ln1_k(cg, a, b),
                              md == 0, md == NCG - 1)

            ln_finish(ln1_st,
                      lambda cg, a, b: ln1_v(cg, a, b),
                      lambda cg, a, b: ln1_k(cg, a, b),
                      lambda cg, a, b: x_v(cg, a, b),
                      lambda cg, a, b: x_k(cg, a, b), EPS_LN,
                      extra=(lambda cg, a, b: u_v(cg, a, b),
                             lambda cg, a, b: u_k(cg, a, b)))

            for half in range(2):
                for j in range(16):
                    m2 = half * 16 + j
                    slot = slab_small("w1", m2)
                    p2 = [ps_next(), ps_next()]
                    mm_group(K("wsm", slot), wsm[slot], NCG,
                             lambda k, a, b: u_v(k, a, b),
                             lambda k, a, b: u_k(k, a, b), G_MAIN, p2)
                    for gi, (c0, c1) in enumerate(G_MAIN):
                        add("s", (lambda e, p=ps[p2[gi]], d=prod_v(j, c0, c1),
                                  bv=sv["b1v"][0][:, m2:m2 + 1], w=c1 - c0:
                                  e.activation(d, p[:, 0:w], AF.Relu, bias=bv)),
                            reads=KP(p2[gi]) + K("b1v"),
                            writes=prod_k(j, c0, c1))
                if half == 1:
                    set_pool(SCR_MAIN)
                    ln2_st = norm_stats_begin(True)
                for md in range(NCG):
                    slot = slab_big("w2", md, half)
                    p2 = [ps_next(), ps_next()]
                    mm_group(K("wbg", slot), wbg[slot], 16,
                             lambda k, a, b: prod_v(k, a, b),
                             lambda k, a, b: prod_k(k, a, b), G_MAIN, p2)
                    for gi, (c0, c1) in enumerate(G_MAIN):
                        if half == 0:
                            add("v", (lambda e, p=ps[p2[gi]],
                                      d=h_v(md, c0, c1), w=c1 - c0:
                                      e.tensor_copy(d, p[:, 0:w])),
                                reads=KP(p2[gi]), writes=h_k(md, c0, c1))
                        else:
                            add("v", (lambda e, p=ps[p2[gi]],
                                      d=h_v(md, c0, c1), w=c1 - c0:
                                      e.tensor_tensor(d, p[:, 0:w], d, ALU.add)),
                                reads=KP(p2[gi]) + h_k(md, c0, c1),
                                writes=h_k(md, c0, c1))
                    if half == 1:
                        add("v", (lambda e, d=h_v(md), bv=sv["b2v"][0][:, md:md + 1],
                                  x2=x_v(md, 0, CH):
                                  e.scalar_tensor_tensor(d, d, bv, x2,
                                                         ALU.add, ALU.add)),
                            reads=h_k(md) + K("b2v") + x_k(md, 0, CH),
                            writes=h_k(md))
            if head_next is not None:
                head_next()
                set_pool(SCR_MAIN)
            for md in range(NCG):
                norm_stats_cg(ln2_st, md,
                              lambda cg, a, b: h_v(cg, a, b),
                              lambda cg, a, b: h_k(cg, a, b),
                              md == 0, md == NCG - 1)

            ln_finish(ln2_st,
                      lambda cg, a, b: h_v(cg, a, b),
                      lambda cg, a, b: h_k(cg, a, b),
                      lambda cg, a, b: SCR_XC[cg % 4][0](a, b),
                      lambda cg, a, b: SCR_XC[cg % 4][1](a, b), EPS_LN,
                      stream_out=ys[pi])

        pass_head(0)
        do_pass(0, head_next=lambda: pass_head(1))
        do_pass(1)

        P.number()
        block = E(nc.Block())

        @block.tensor
        def _(t):
            P.emit_engine("pe", t)

        @block.vector
        def _(v):
            P.emit_engine("v", v)

        @block.scalar
        def _(s):
            P.emit_engine("s", s)

        @block.gpsimd
        def _(g):
            P.emit_engine("p", g)

        @block.sync
        def _(q):
            P.emit_engine("q", q)
            q.wait_ge(dsem["y0"], P.dma_counts.get("y0", 0))
            q.wait_ge(dsem["y1"], P.dma_counts.get("y1", 0))

    return nc


def _tile_w(w):
    Kd, Md = w.shape
    t = w.reshape(Kd // 128, 128, Md // 128, 128).transpose(2, 1, 0, 3)
    return np.ascontiguousarray(t.reshape(Md // 128 * 128, Kd))


def _to_fm(v):
    return np.ascontiguousarray(v.reshape(-1, 128).T)


_NC_CACHE = {}


def make_in_maps(L0_attn_x, L0_attn_norm1_w, L0_attn_Wx, L0_attn_Wy,
                 L0_attn_conv_w, L0_attn_conv_b, L0_attn_Wi, L0_attn_Wa,
                 L0_attn_lam, L0_attn_Wo, L0_attn_norm2_w, L0_attn_Wg,
                 L0_attn_Wu, L0_attn_Wd, L0_ffn_W1, L0_ffn_b1, L0_ffn_W2,
                 L0_ffn_b2):
    x = np.asarray(L0_attn_x, dtype=np.float32)

    shared = dict(
        wx=_tile_w(np.asarray(L0_attn_Wx, np.float32)),
        wy=_tile_w(np.asarray(L0_attn_Wy, np.float32)),
        wa=_tile_w(np.asarray(L0_attn_Wa, np.float32)),
        wi=_tile_w(np.asarray(L0_attn_Wi, np.float32)),
        wo=_tile_w(np.asarray(L0_attn_Wo, np.float32)),
        wg=_tile_w(np.asarray(L0_attn_Wg, np.float32)),
        wu=_tile_w(np.asarray(L0_attn_Wu, np.float32)),
        wd=_tile_w(np.asarray(L0_attn_Wd, np.float32)),
        w1=_tile_w(np.asarray(L0_ffn_W1, np.float32)),
        w2=_tile_w(np.asarray(L0_ffn_W2, np.float32)),
        norm1w=_to_fm(np.asarray(L0_attn_norm1_w, np.float32)),
        norm2w=_to_fm(np.asarray(L0_attn_norm2_w, np.float32)),
        convb=_to_fm(np.asarray(L0_attn_conv_b, np.float32)),
        convw=np.ascontiguousarray(np.concatenate(
            [_to_fm(np.asarray(L0_attn_conv_w, np.float32)[k])
             for k in range(4)], axis=1)),
        lam=_to_fm(np.asarray(L0_attn_lam, np.float32)),
        b1v=_to_fm(np.asarray(L0_ffn_b1, np.float32)),
        b2v=_to_fm(np.asarray(L0_ffn_b2, np.float32)),
        ident=np.eye(128, dtype=np.float32),
    )

    in_maps = []
    for c in range(N_CORES):
        pair, role = c // 2, c % 2
        m = dict(shared)
        for pi in range(2):
            ck = 2 * pi + role
            t0 = ck * CH
            if t0 == 0:
                sl = np.concatenate(
                    [np.zeros((3, D), np.float32), x[pair, 0:CH, :]], axis=0)
            else:
                sl = x[pair, t0 - 3:t0 + CH, :]
            m[f"x{pi}"] = np.ascontiguousarray(sl.T)
        me = np.zeros((128, 128), np.float32)
        mo = np.zeros((128, 128), np.float32)
        me[:, (2 * pair) * 16:(2 * pair + 1) * 16] = 1.0
        mo[:, (2 * pair + 1) * 16:(2 * pair + 2) * 16] = 1.0
        m["mask_e"] = me
        m["mask_o"] = mo
        sl2 = np.zeros((128, 2), np.float32)
        sl2[:, 0] = float(role)
        sl2[:, 1] = 1.0 - float(role)
        m["slf"] = sl2
        in_maps.append(m)
    return in_maps


def assemble(results):
    out = np.empty((B, SEQ, D), dtype=np.float32)
    for c in range(N_CORES):
        pair, role = c // 2, c % 2
        for pi in range(2):
            ck = 2 * pi + role
            y = results[c][f"y{pi}"]
            out[pair, ck * CH:(ck + 1) * CH, :] = y.T
    return out


def kernel(**inputs):
    in_maps = make_in_maps(**inputs)
    if "nc" not in _NC_CACHE:
        _NC_CACHE["nc"] = build_nc()
    nc = _NC_CACHE["nc"]
    res = run_bass_kernel_spmd(nc, in_maps, core_ids=list(range(N_CORES)))
    return assemble(res.results)


def _dead(results):
    out = np.empty((B, SEQ, D), dtype=np.float32)
    return out
